# version 09
# baseline (speedup 1.0000x reference)
import numpy as np
import ml_dtypes
import contextlib
import concourse.bass as bass
import concourse.mybir as mybir
from concourse.bass_utils import run_bass_kernel_spmd

F32 = mybir.dt.float32
BF16 = mybir.dt.bfloat16
AF = mybir.ActivationFunctionType
ALU = mybir.AluOpType
BF = ml_dtypes.bfloat16
AXX = mybir.AxisListType.X

D = 1024
DI = 2048
NH = 16
DC = 4
Q = 128
QT = 512
HT = 1024
NKC = 8
NIC = 16
NE = 8
LN_EPS = 1e-5
RMS_EPS = 1.1920928955078125e-07
NEG_BIG = 1e30


def build_program(n_half=2):
    nc = bass.Bass("TRN2", target_bir_lowering=False, debug=False)
    LT = n_half * HT

    u_t = nc.dram_tensor("u_t", [D, LT], F32, kind="ExternalInput").ap()
    out_t = nc.dram_tensor("out_t", [D, LT], F32, kind="ExternalOutput").ap()
    win_x = nc.dram_tensor("win_x", [NIC, 128, 1024], F32, kind="ExternalInput").ap()
    win_b = nc.dram_tensor("win_b", [NH, 128, 512], F32, kind="ExternalInput").ap()
    win_c = nc.dram_tensor("win_c", [NH, 128, 512], F32, kind="ExternalInput").ap()
    win_dt = nc.dram_tensor("win_dt", [128, 128], F32, kind="ExternalInput").ap()
    wgate = nc.dram_tensor("wgate", [NIC, 128, 1024], F32, kind="ExternalInput").ap()
    wout = nc.dram_tensor("wout", [NKC, 128, 2048], F32, kind="ExternalInput").ap()
    wrout = nc.dram_tensor("wrout", [128, 64], F32, kind="ExternalInput").ap()
    w1t = nc.dram_tensor("w1t", [NE, 128, 65536], F32, kind="ExternalInput").ap()
    w2t = nc.dram_tensor("w2t", [NE, 4, 128, 8192], F32, kind="ExternalInput").ap()
    c_idbf = nc.dram_tensor("c_idbf", [128, 128], BF16, kind="ExternalInput").ap()
    c_idf = nc.dram_tensor("c_idf", [128, 128], F32, kind="ExternalInput").ap()
    c_tri01 = nc.dram_tensor("c_tri01", [128, 128], F32, kind="ExternalInput").ap()
    c_trimask = nc.dram_tensor("c_trimask", [128, 128], F32, kind="ExternalInput").ap()
    c_onesrow = nc.dram_tensor("c_onesrow", [1, 128], F32, kind="ExternalInput").ap()
    c_onescol = nc.dram_tensor("c_onescol", [128, 1], BF16, kind="ExternalInput").ap()
    c_cc = nc.dram_tensor("c_cc", [128, 4], F32, kind="ExternalInput").ap()
    c_normw = nc.dram_tensor("c_normw", [128, 16], F32, kind="ExternalInput").ap()
    c_convb = nc.dram_tensor("c_convb", [128, 16], F32, kind="ExternalInput").ap()
    c_convw = nc.dram_tensor("c_convw", [128, 64], F32, kind="ExternalInput").ap()
    c_dptile = nc.dram_tensor("c_dptile", [128, 64], F32, kind="ExternalInput").ap()
    c_routb = nc.dram_tensor("c_routb", [8, 1], F32, kind="ExternalInput").ap()

    es = contextlib.ExitStack()
    with es:
        sb = lambda name, shape, dt: es.enter_context(nc.sbuf_tensor(name, shape, dt))
        acc = sb("acc", [128, NKC * HT], F32)
        xn = sb("xn", [128, NKC * HT], BF16)
        A1 = sb("A1", [128, 32 * HT], BF16)
        stg = sb("stg", [128, 2 * 2048], F32)
        wbf = sb("wbf", [128, 2 * 2048], BF16)
        stgw2 = sb("stgw2", [128, 1024], F32)
        w2bf = sb("w2bf", [128, 1024], BF16)
        x_pre = sb("x_pre", [128, QT + 3], F32)
        tails = sb("tails", [128, NIC * 3], F32)
        SBm = sb("SBm", [128, NH * Q], F32)
        expSB = sb("expSB", [128, NH * Q], BF16)
        alph = sb("alph", [128, 16], F32)
        dtw = sb("dtw", [128, 4 * 64], F32)
        nsc_tok = sb("nsc_tok", [128, 64], F32)
        nsc_hm = sb("nsc_hm", [64, 128], F32)
        w_hm = sb("w_hm", [64, 128], F32)
        w_tok = sb("w_tok", [128, 64], F32)
        nb16 = sb("nb16", [64, 1], F32)
        Hp = sb("Hp", [128, 8 * 128], F32)
        t_xtok = sb("t_xtok", [128, 256], BF16)
        t_wb = sb("t_wb", [128, 256], BF16)
        t_mp = sb("t_mp", [128, 256], BF16)
        t_cs = sb("t_cs", [128, 256], BF16)
        t_e = sb("t_e", [128, 512], F32)
        t_hbf = sb("t_hbf", [128, 256], BF16)
        rowbuf = sb("rowbuf", [1, 512], F32)
        s_T = sb("s_T", [32, HT], BF16)
        s_b2 = sb("s_b2", [128, 2 * HT], BF16)
        shtmp = sb("shtmp", [32, HT], BF16)
        ln_t = sb("ln_t", [1, 3 * QT], F32)
        rms_t = sb("rms_t", [1, QT], F32)
        tmp1024 = sb("tmp1024", [128, 1024], F32)
        ubf = sb("ubf", [128, 2 * 512], BF16)
        usq = sb("usq", [128, 2 * 512], BF16)
        ysq_t = sb("ysq_t", [128, 512], BF16)
        lt_r = sb("lt_r", [8, HT], F32)
        e_t = sb("e_t", [128, 8], F32)
        mask_t = sb("mask_t", [128, 8], F32)
        r_sc = sb("r_sc", [128, 4], F32)
        idbf = sb("idbf", [128, 128], BF16)
        idf = sb("idf", [128, 128], F32)
        tri01 = sb("tri01", [128, 128], F32)
        trimask = sb("trimask", [128, 128], F32)
        onesrow = sb("onesrow", [1, 128], F32)
        onescol = sb("onescol", [128, 1], BF16)
        cc = sb("cc", [128, 4], F32)
        normw = sb("normw", [128, 16], F32)
        convb = sb("convb", [128, 16], F32)
        convw = sb("convw", [128, 64], F32)
        dptile = sb("dptile", [128, 64], F32)
        routb = sb("routb", [8, 1], F32)
        wrout_s = sb("wrout_s", [128, 64], F32)
        wrout_bf = sb("wrout_bf", [128, 64], BF16)
        wdt_bf = sb("wdt_bf", [128, 128], BF16)

        sems = {}
        for name in ("dW0", "dW1", "dW2", "dW3", "dW4", "dW5", "dG", "dS", "pe", "ve", "ac", "gp"):
            sems[name] = es.enter_context(nc.semaphore(name))
        block = es.enter_context(nc.Block())

        C = {k: 0 for k in sems}
        _updated = set()
        _updated_refs = []
        OWN = {"tensor": "pe", "scalar": "ac", "vector": "ve", "gpsimd": "gp"}

        def inc(ins, s, amt=1):
            assert id(ins) not in _updated, "instruction already carries an update"
            _updated.add(id(ins))
            _updated_refs.append(ins)
            ins.then_inc(sems[s], amt)
            C[s] += amt
            return C[s]

        def sect(engname, fn):
            deco = getattr(block, engname)

            @deco
            def _(eng):
                fn(eng)

        def note(engname, ins):
            if id(ins) not in _updated:
                inc(ins, OWN[engname])
            return ins

        def barrier():
            tgt = {s: C[s] for s in ("pe", "ve", "ac", "gp")}

            def f(e):
                for s in ("pe", "ve", "ac", "gp"):
                    if tgt[s] > 0:
                        e.wait_ge(sems[s], tgt[s])
            for engname in ("tensor", "scalar", "vector", "gpsimd"):
                sect(engname, f)

        state = {"prev_ffn_ev": 0, "prev_ffn_mm": 0, "stg_cast_last": ("ac", 0),
                 "ssd_prev_ac": 0, "ssd_prev_yg": 0, "ssd_prev_d4": 0,
                 "s_T_ready": 0, "sb2_ready": 0, "sb2_guard": [0, 0],
                 "prev_ffn_ev_pair": [0, 0], "prev_w2_cast": 0, "prev_ffn_hg": 0,
                 "shtmp_guard": 0}
        _uid = [0]

        def pt(name, shape, dt):
            _uid[0] += 1
            return nc.psum_tensor(f"{name}_{_uid[0]}", shape, dt)

        def init_phase():
            def f_g(g):
                for dst, src in [(idbf, c_idbf), (idf, c_idf), (tri01, c_tri01),
                                 (trimask, c_trimask), (onesrow, c_onesrow),
                                 (onescol, c_onescol), (cc, c_cc), (normw, c_normw),
                                 (convb, c_convb), (convw, c_convw), (dptile, c_dptile),
                                 (routb, c_routb), (wrout_s, wrout)]:
                    inc(g.dma_start(out=dst[:, :], in_=src[:, :]), "dG", 16)
                inc(g.dma_start(out=stg[:, 0:128], in_=win_dt[:, :]), "dG", 16)
            sect("gpsimd", f_g)
            dg0 = C["dG"]

            def f_s(s):
                s.wait_ge(sems["dG"], dg0)
                s.copy(wdt_bf[:, :], stg[:, 0:128])
                ins = s.copy(wrout_bf[:, :], wrout_s[:, :])
                state["stg_cast_last"] = ("ac", inc(ins, "ac"))
                note("scalar", ins)
            sect("scalar", f_s)

            def f_v(v):
                v.wait_ge(sems["dG"], dg0)
                v.memset(Hp[:, :], 0.0)
                v.memset(s_T[:, :], 0.0)
                v.memset(s_b2[:, :], 0.0)
                note("vector", v.memset(tails[:, :], 0.0))
            sect("vector", f_v)
            barrier()

        def ln_phase(half, q):
            with (pt("p_s1", [1, 512], F32) as p_s1,
                  pt("p_s2", [1, 512], F32) as p_s2,
                  pt("p_rb", [128, 512], F32) as p_rb,
                  pt("p_mrb", [128, 512], F32) as p_mrb):
                ts0 = q * QT
                if q == 0:
                    st_w = C["dS"]
                    dg_before = C["dG"]

                    def f_g(g):
                        g.wait_ge(sems["dS"], st_w)
                        g.wait_ge(sems["dG"], dg_before)
                        for cch in range(NKC):
                            inc(g.dma_start(
                                out=acc[:, cch * HT:(cch + 1) * HT],
                                in_=u_t[cch * 128:(cch + 1) * 128,
                                        half * HT:(half + 1) * HT]), "dG", 16)
                    sect("gpsimd", f_g)
                dg_u = C["dG"]
                pe_base = C["pe"]
                cast_v = []

                def f_v(v):
                    v.wait_ge(sems["dG"], dg_u)
                    for cch in range(NKC):
                        sl = acc[:, cch * HT + ts0: cch * HT + ts0 + QT]
                        p = cch % 2
                        if cch >= 2:
                            v.wait_ge(sems["pe"], pe_base + (cch - 2) + 1)
                        v.tensor_copy(ubf[:, p * 512:(p + 1) * 512], sl)
                        ins = v.tensor_tensor(usq[:, p * 512:(p + 1) * 512], sl, sl,
                                              op=ALU.mult)
                        cast_v.append(inc(ins, "ve"))
                sect("vector", f_v)

                def f_t(t):
                    for cch in range(NKC):
                        p = cch % 2
                        t.wait_ge(sems["ve"], cast_v[cch])
                        t.matmul(p_s1[:, :], onescol[:, :], ubf[:, p * 512:(p + 1) * 512],
                                 start=(cch == 0), stop=(cch == NKC - 1))
                        ins = t.matmul(p_s2[:, :], onescol[:, :],
                                       usq[:, p * 512:(p + 1) * 512],
                                       start=(cch == 0), stop=(cch == NKC - 1))
                        inc(ins, "pe")
                sect("tensor", f_t)
                pe_mm = C["pe"]

                def f_v2(v):
                    v.wait_ge(sems["pe"], pe_mm)
                    m, vv, r = (ln_t[:, i * QT:(i + 1) * QT] for i in range(3))
                    v.tensor_scalar(m, p_s1[:, :], 1.0 / D, None, op0=ALU.mult)
                    v.tensor_scalar(vv, p_s2[:, :], 1.0 / D, None, op0=ALU.mult)
                    v.drain()
                    v.tensor_tensor(r, m, m, op=ALU.mult)
                    v.drain()
                    inc(v.tensor_tensor(vv, vv, r, op=ALU.subtract), "ve")
                sect("vector", f_v2)
                ve_st = C["ve"]

                def f_s(s):
                    s.wait_ge(sems["ve"], ve_st)
                    inc(s.activation(ln_t[:, 2 * QT:], ln_t[:, QT:2 * QT], AF.Sqrt,
                                     bias=cc[0:1, 1:2], scale=1.0), "ac")
                sect("scalar", f_s)
                ac_sq = C["ac"]

                def f_v3(v):
                    v.wait_ge(sems["ac"], ac_sq)
                    r, m = ln_t[:, 2 * QT:], ln_t[:, 0:QT]
                    v.reciprocal(r, r)
                    v.drain()
                    inc(v.tensor_tensor(m, m, r, op=ALU.mult), "ve")
                sect("vector", f_v3)
                ve_r = C["ve"]

                def f_t2(t):
                    t.wait_ge(sems["ve"], ve_r)
                    t.matmul(p_rb[:, :], onesrow[:, :], ln_t[0:1, 2 * QT:],
                             start=True, stop=True)
                    inc(t.matmul(p_mrb[:, :], onesrow[:, :], ln_t[0:1, 0:QT],
                                 start=True, stop=True), "pe")
                sect("tensor", f_t2)
                pe_bc = C["pe"]

                def f_v4(v):
                    v.wait_ge(sems["pe"], pe_bc)
                    for cch in range(NKC):
                        sl = acc[:, cch * HT + ts0: cch * HT + ts0 + QT]
                        xsl = xn[:, cch * HT + ts0: cch * HT + ts0 + QT]
                        v.tensor_tensor(tmp1024[:, 0:QT], sl, p_rb[:, :], op=ALU.mult)
                        v.drain()
                        note("vector", v.tensor_tensor(xsl, tmp1024[:, 0:QT],
                                                       p_mrb[:, :], op=ALU.subtract))
                        v.drain()
                sect("vector", f_v4)
                barrier()

        def z_phase(half, q):
            with (pt("p_z0", [128, 512], F32) as p_z0,
                  pt("p_z1", [128, 512], F32) as p_z1,
                  pt("p_dt", [128, 64], F32) as p_dt):
                p_z = [p_z0, p_z1]
                ts0 = q * QT
                xn_sl = lambda k: xn[:, k * HT + ts0: k * HT + ts0 + QT]

                dma_d, cast_a, mm_pe, evac_v, silu_a = [], [], [], [], []
                for m in range(NIC):
                    p = m % 2

                    def f_q(g, m=m, p=p):
                        if m >= 2:
                            g.wait_ge(sems["gp"], cast_a[m - 2])
                        else:
                            g.wait_ge(sems[state["stg_cast_last"][0]], state["stg_cast_last"][1])
                        ins = g.dma_start(out=stg[:, p * 2048: p * 2048 + 1024],
                                          in_=win_x[m, :, :])
                        dma_d.append(inc(ins, f"dW{p}", 16))
                    sect("sync", f_q)

                    def f_c(g, m=m, p=p):
                        g.wait_ge(sems[f"dW{p}"], dma_d[m])
                        if m >= 2:
                            g.wait_ge(sems["pe"], mm_pe[m - 2])
                        ins = g.tensor_copy(wbf[:, p * 2048: p * 2048 + 1024],
                                            stg[:, p * 2048: p * 2048 + 1024])
                        cast_a.append(inc(ins, "gp"))
                    sect("gpsimd", f_c)

                    def f_t(t, m=m, p=p):
                        t.wait_ge(sems["gp"], cast_a[m])
                        if m >= 2:
                            t.wait_ge(sems["ve"], evac_v[m - 2])
                        for k in range(NKC):
                            ins = t.matmul(p_z[p][:, :],
                                           wbf[:, p * 2048 + k * 128:
                                               p * 2048 + (k + 1) * 128],
                                           xn_sl(k), start=(k == 0), stop=(k == NKC - 1))
                        mm_pe.append(inc(ins, "pe"))
                    sect("tensor", f_t)

                    def f_v(v, m=m, p=p):
                        v.wait_ge(sems["pe"], mm_pe[m])
                        if m >= 1:
                            v.wait_ge(sems["ac"], silu_a[m - 1])
                        v.tensor_copy(x_pre[:, 3:3 + QT], p_z[p][:, :])
                        v.drain()
                        v.tensor_copy(x_pre[:, 0:3], tails[:, m * 3:(m + 1) * 3])
                        v.drain()
                        v.tensor_copy(tails[:, m * 3:(m + 1) * 3], x_pre[:, QT:QT + 3])
                        v.drain()
                        v.tensor_scalar(tmp1024[:, 0:QT], x_pre[:, 0:QT],
                                        convw[:, m * 4: m * 4 + 1], None, op0=ALU.mult)
                        ins = None
                        for kk in range(1, DC):
                            v.drain()
                            ins = v.scalar_tensor_tensor(
                                tmp1024[:, 0:QT], x_pre[:, kk:kk + QT],
                                convw[:, m * 4 + kk: m * 4 + kk + 1],
                                tmp1024[:, 0:QT], op0=ALU.mult, op1=ALU.add)
                        evac_v.append(inc(ins, "ve"))
                    sect("vector", f_v)

                    def f_s2(s, m=m):
                        s.wait_ge(sems["ve"], evac_v[m])
                        ins = s.activation(A1[:, 24 * HT + m * QT: 24 * HT + (m + 1) * QT],
                                           tmp1024[:, 0:QT],
                                           AF.Silu, bias=convb[:, m:m + 1])
                        silu_a.append(inc(ins, "ac"))
                        note("scalar", ins)
                    sect("scalar", f_s2)

                bc_dma, bc_cast, bc_mm, bc_ev = [], [], [], []
                for idx in range(NH):
                    isC = idx >= 8
                    j = idx % 8
                    src = win_c if isC else win_b
                    p = idx % 2

                    def f_q(g, idx=idx, j=j, src=src, p=p):
                        if idx >= 2:
                            g.wait_ge(sems["ve"], bc_cast[idx - 2])
                        else:
                            g.wait_ge(sems["gp"], cast_a[-1])
                        g.dma_start(out=stg[:, p * 2048: p * 2048 + 512],
                                    in_=src[2 * j, :, :]).then_inc(sems[f"dW{p}"], 16)
                        C[f"dW{p}"] += 16
                        ins = g.dma_start(out=stg[:, p * 2048 + 512: p * 2048 + 1024],
                                          in_=src[2 * j + 1, :, :])
                        bc_dma.append(inc(ins, f"dW{p}", 16))
                    sect("sync", f_q)

                    def f_c(v, idx=idx, p=p):
                        v.wait_ge(sems[f"dW{p}"], bc_dma[idx])
                        if idx >= 2:
                            v.wait_ge(sems["pe"], bc_mm[idx - 2])
                        else:
                            v.wait_ge(sems["pe"], mm_pe[-1])
                        ins = v.tensor_copy(wbf[:, p * 2048: p * 2048 + 1024],
                                            stg[:, p * 2048: p * 2048 + 1024])
                        bc_cast.append(inc(ins, "ve"))
                    sect("vector", f_c)

                    def f_t(t, idx=idx, p=p):
                        t.wait_ge(sems["ve"], bc_cast[idx])
                        if idx >= 2:
                            t.wait_ge(sems["ve"], bc_ev[idx - 2])
                        else:
                            t.wait_ge(sems["ve"], evac_v[-1])
                        ins = None
                        for hh in range(2):
                            for k in range(NKC):
                                ins = t.matmul(
                                    p_z[p][hh * 64:(hh + 1) * 64, :],
                                    wbf[:, p * 2048 + hh * 512 + k * 64:
                                        p * 2048 + hh * 512 + (k + 1) * 64],
                                    xn_sl(k), start=(k == 0), stop=(k == NKC - 1),
                                    tile_position=(0, hh * 64))
                        bc_mm.append(inc(ins, "pe"))
                    sect("tensor", f_t)

                    def f_v(v, idx=idx, j=j, isC=isC, p=p):
                        v.wait_ge(sems["pe"], bc_mm[idx])
                        base = (8 * QT if isC else 0) + j * QT
                        ins = v.tensor_copy(A1[:, base: base + QT], p_z[p][:, :])
                        bc_ev.append(inc(ins, "ve"))
                    sect("vector", f_v)

                dt_pe, dt_ve = [], []
                for tc in range(QT // 128):
                    def f_t2(t, tc=tc):
                        if tc >= 1:
                            t.wait_ge(sems["ve"], dt_ve[tc - 1])
                        ins = None
                        for k in range(NKC):
                            ins = t.matmul(
                                p_dt[:, 0:16],
                                xn[:, k * HT + ts0 + tc * 128:
                                   k * HT + ts0 + (tc + 1) * 128],
                                wdt_bf[:, k * 16:(k + 1) * 16],
                                start=(k == 0), stop=(k == NKC - 1))
                        dt_pe.append(inc(ins, "pe"))
                    sect("tensor", f_t2)

                    def f_v2a(v, tc=tc):
                        v.wait_ge(sems["pe"], dt_pe[tc])
                        ins = v.tensor_copy(dtw[:, tc * 16:(tc + 1) * 16], p_dt[:, 0:16])
                        dt_ve.append(inc(ins, "ve"))
                    sect("vector", f_v2a)

                sl_ac = silu_a[-1]

                def f_v2(v):
                    v.wait_ge(sems["ve"], dt_ve[-1])
                    v.wait_ge(sems["ac"], sl_ac)
                    d0, d1 = dtw[:, 0:64], dtw[:, 64:128]
                    t_mx = tmp1024[:, 64:128]
                    v.drain()
                    v.tensor_tensor(d1, d0, dptile[:, :], op=ALU.add)
                    v.drain()
                    inc(v.tensor_scalar(t_mx, d1, 0.0, None, op0=ALU.max), "ve")
                sect("vector", f_v2)
                ve_dt = C["ve"]

                def f_s3(s):
                    s.wait_ge(sems["ve"], ve_dt)
                    d0, d1, d2 = (dtw[:, i * 64:(i + 1) * 64] for i in range(3))
                    t_ax = tmp1024[:, 0:64]
                    s.activation(d2, d0, AF.Abs)
                    s.activation(t_ax, d1, AF.Abs)
                    s.drain()
                    s.activation(t_ax, t_ax, AF.Exp, scale=-1.0,
                                 bias=cc[0:128, 3:4])
                    s.drain()
                    inc(s.activation(t_ax, t_ax, AF.Ln, bias=cc[0:128, 0:1]), "ac")
                sect("scalar", f_s3)
                ac_sp = C["ac"]

                def f_v3(v):
                    v.wait_ge(sems["ac"], ac_sp)
                    d1, d2, d3 = (dtw[:, i * 64:(i + 1) * 64] for i in range(1, 4))
                    t_ax = tmp1024[:, 0:64]
                    t_mx = tmp1024[:, 64:128]
                    v.tensor_tensor(d1, t_ax, t_mx, op=ALU.add)
                    v.drain()
                    note("vector", v.tensor_tensor(d3, d1, d2, op=ALU.mult))
                sect("vector", f_v3)
                state["stg_cast_last"] = ("ve", bc_cast[-1])
                barrier()

        def gate_phase(half, q):
            with (pt("p_z0", [128, 512], F32) as p_z0,
                  pt("p_z1", [128, 512], F32) as p_z1):
                p_z = [p_z0, p_z1]
                ts0 = q * QT
                xn_sl = lambda k: xn[:, k * HT + ts0: k * HT + ts0 + QT]
                g_dma, g_cast, g_mm, g_ev = [], [], [], []
                for m in range(NIC):
                    p = m % 2

                    def f_q(g, m=m, p=p):
                        if m >= 2:
                            g.wait_ge(sems["ve"], g_cast[m - 2])
                        else:
                            g.wait_ge(sems[state["stg_cast_last"][0]], state["stg_cast_last"][1])
                        ins = g.dma_start(out=stg[:, p * 2048: p * 2048 + 1024],
                                          in_=wgate[m, :, :])
                        g_dma.append(inc(ins, f"dW{p}", 16))
                    sect("sync", f_q)

                    def f_c(v, m=m, p=p):
                        v.wait_ge(sems[f"dW{p}"], g_dma[m])
                        if m >= 2:
                            v.wait_ge(sems["pe"], g_mm[m - 2])
                        ins = v.tensor_copy(wbf[:, p * 2048: p * 2048 + 1024],
                                            stg[:, p * 2048: p * 2048 + 1024])
                        g_cast.append(inc(ins, "ve"))
                    sect("vector", f_c)

                    def f_t(t, m=m, p=p):
                        t.wait_ge(sems["ve"], g_cast[m])
                        if m >= 2:
                            t.wait_ge(sems["ac"], g_ev[m - 2])
                        ins = None
                        for k in range(NKC):
                            ins = t.matmul(p_z[p][:, :],
                                           wbf[:, p * 2048 + k * 128:
                                               p * 2048 + (k + 1) * 128],
                                           xn_sl(k), start=(k == 0), stop=(k == NKC - 1))
                        g_mm.append(inc(ins, "pe"))
                    sect("tensor", f_t)

                    def f_s2(s, m=m, p=p):
                        s.wait_ge(sems["pe"], g_mm[m])
                        ins = s.activation(A1[:, (16 + m) * QT:(17 + m) * QT],
                                           p_z[p][:, :], AF.Sigmoid)
                        g_ev.append(inc(ins, "ac"))
                        note("scalar", ins)
                    sect("scalar", f_s2)
                state["stg_cast_last"] = ("ve", g_cast[-1])
                barrier()

        def ssd_phase(half, q):
            with (pt("pY0", [128, 128], F32) as pY0,
                  pt("pY1", [128, 128], F32) as pY1,
                  pt("pCB0", [128, 128], F32) as pCB0,
                  pt("pCB1", [128, 128], F32) as pCB1,
                  pt("p_sb", [128, 512], F32) as p_sb,
                  pt("pXt", [128, 128], BF16) as pXt,
                  pt("pBt", [128, 128], BF16) as pBt,
                  pt("pdH", [128, 128], F32) as pdH):
                pYs = [pY0, pY1]
                pCBs = [pCB0, pCB1]
                negla4 = dtw[:, 192:256]

                def f_t1(t, w=state["ssd_prev_yg"]):
                    t.wait_ge(sems["ve"], w)
                    inc(t.matmul(pY0[:, 0:64], tri01[:, :], negla4,
                                 start=True, stop=True), "pe")
                sect("tensor", f_t1)
                pe_c1 = C["pe"]

                def f_v1(v, wac=state["ssd_prev_ac"]):
                    v.wait_ge(sems["pe"], pe_c1)
                    v.wait_ge(sems["ac"], wac)
                    inc(v.tensor_copy(nsc_tok[:, :], pY0[:, 0:64]), "ve")
                sect("vector", f_v1)
                ve_d1 = C["ve"]

                def f_t2(t):
                    t.wait_ge(sems["ve"], ve_d1)
                    inc(t.transpose(pY1[0:64, :], nsc_tok[:, :], idf[:, :]), "pe")
                sect("tensor", f_t2)
                pe_c2 = C["pe"]

                def f_v2(v, wac=state["ssd_prev_ac"]):
                    v.wait_ge(sems["pe"], pe_c2)
                    v.wait_ge(sems["ac"], wac)
                    v.tensor_copy(nsc_hm[:, :], pY1[0:64, :])
                    v.drain()
                    inc(v.tensor_scalar(nb16[:, :], nsc_hm[:, 127:128], -1.0, None,
                                        op0=ALU.mult), "ve")
                sect("vector", f_v2)
                ve_d2 = C["ve"]

                def f_s1(s):
                    s.wait_ge(sems["ve"], ve_d2)
                    inc(s.activation(w_hm[:, :], nsc_hm[:, :], AF.Exp,
                                     bias=nb16[:, :], scale=1.0), "ac")
                sect("scalar", f_s1)
                ac_w = C["ac"]

                def f_t3(t):
                    t.wait_ge(sems["ac"], ac_w)
                    inc(t.transpose(pCB0[:, 0:64], w_hm[:, :], idf[0:64, 0:64]), "pe")
                sect("tensor", f_t3)
                pe_c4 = C["pe"]

                def f_v3(v):
                    v.wait_ge(sems["pe"], pe_c4)
                    inc(v.tensor_copy(w_tok[:, :], pCB0[:, 0:64]), "ve")
                sect("vector", f_v3)
                ve_d4q = C["ve"]
                state["ssd_prev_d4"] = ve_d4q

                e_ac_prev_chunk = state["ssd_prev_ac"]
                pv_xtok = ve_d4q
                pv_wb_free = ve_d4q
                pv_hup = ve_d4q
                ve_wb = ve_d4q
                prev_mp_last = ve_d4q
                prev_pe_sb = C["pe"]
                hist_mp = [ve_d4q, ve_d4q]
                hist_yg = [ve_d1, ve_d2]
                for kq in range(QT // Q):
                    pe_sb, sb_ve, sb_ac = [], [], []
                    mp_hist = []
                    for qq in range(4):
                        def f_g(g, kq=kq, qq=qq, prev_pe_sb=prev_pe_sb):
                            g.wait_ge(sems["ve"], ve_d2)
                            if qq >= 1:
                                g.wait_ge(sems["pe"], pe_sb[qq - 1])
                            else:
                                g.wait_ge(sems["pe"], prev_pe_sb)
                            inc(g.dma_start(
                                out=rowbuf[0:1, 0:512],
                                in_=nsc_hm[kq * 16 + qq * 4: kq * 16 + (qq + 1) * 4, :]),
                                "dG", 16)
                        sect("gpsimd", f_g)
                        dg_f = C["dG"]

                        def f_t4(t, qq=qq, dg_f=dg_f):
                            t.wait_ge(sems["dG"], dg_f)
                            if qq >= 1:
                                t.wait_ge(sems["ve"], sb_ve[qq - 1])
                                t.wait_ge(sems["ac"], sb_ac[qq - 1])
                            ins = t.matmul(p_sb[:, :], onesrow[:, :],
                                           rowbuf[0:1, 0:512],
                                           start=True, stop=True)
                            pe_sb.append(inc(ins, "pe"))
                        sect("tensor", f_t4)

                        def f_v4(v, kq=kq, qq=qq, wac=e_ac_prev_chunk):
                            v.wait_ge(sems["pe"], pe_sb[qq])
                            v.wait_ge(sems["ac"], wac)
                            v.scalar_tensor_tensor(
                                SBm[:, qq * 512:(qq + 1) * 512], p_sb[:, :], 1.0,
                                trimask[:, :].rearrange("p (a c) -> p a c", a=1)
                                .broadcast_to([128, 4, 128]),
                                op0=ALU.mult, op1=ALU.add)
                            v.tensor_scalar(alph[:, qq * 4:(qq + 1) * 4],
                                            p_sb[:, 127:512:128], 1.0, None,
                                            op0=ALU.mult)
                            v.drain()
                            ins = None
                            for hh in range(4):
                                h = qq * 4 + hh
                                ins = v.tensor_scalar(
                                    SBm[:, h * 128:(h + 1) * 128],
                                    SBm[:, h * 128:(h + 1) * 128],
                                    nsc_tok[:, kq * 16 + h: kq * 16 + h + 1], None,
                                    op0=ALU.subtract)
                            sb_ve.append(inc(ins, "ve"))
                        sect("vector", f_v4)

                        def f_s2(s, qq=qq, wve=state["ssd_prev_yg"], kq=kq,
                                 prev_mp_last=prev_mp_last):
                            s.wait_ge(sems["pe"], pe_sb[qq])
                            s.wait_ge(sems["ve"], wve)
                            s.wait_ge(sems["ve"], sb_ve[qq])
                            s.activation(expSB[:, qq * 512:(qq + 1) * 512],
                                         p_sb[:, :], AF.Exp,
                                         bias=cc[0:128, 3:4], scale=-1.0)
                            s.drain()
                            if qq >= 1:
                                s.wait_ge(sems["ve"], mp_hist[-1])
                            else:
                                s.wait_ge(sems["ve"], prev_mp_last)
                            s.activation(alph[:, qq * 4:(qq + 1) * 4],
                                         alph[:, qq * 4:(qq + 1) * 4], AF.Exp,
                                         bias=cc[0:128, 3:4], scale=-1.0)
                            s.drain()
                            ins = s.activation(t_e[:, :],
                                               SBm[:, qq * 512:(qq + 1) * 512],
                                               AF.Exp, bias=cc[0:128, 3:4],
                                               scale=-1.0)
                            sb_ac.append(inc(ins, "ac"))
                        sect("scalar", f_s2)
                        e_ready = C["ac"]

                        for hh in range(4):
                            h = qq * 4 + hh
                            base = (h % 2) * 64
                            jj = h // 2
                            x_sl = A1[:, 24 * HT + h * QT + kq * 128:
                                      24 * HT + h * QT + (kq + 1) * 128]
                            bcols = slice(jj * QT + kq * 128, jj * QT + (kq + 1) * 128)
                            ccols = slice(8 * QT + jj * QT + kq * 128,
                                          8 * QT + jj * QT + (kq + 1) * 128)
                            gcols = slice((16 + h) * QT + kq * 128,
                                          (16 + h) * QT + (kq + 1) * 128)

                            def f_t5(t, w=pv_xtok, x_sl=x_sl):
                                t.wait_ge(sems["ve"], w)
                                inc(t.transpose(pXt[:, :], x_sl, idbf[:, :]), "pe")
                            sect("tensor", f_t5)
                            pe_tx = C["pe"]

                            def f_v5(v, pe_tx=pe_tx, h=h):
                                v.wait_ge(sems["pe"], pe_tx)
                                slm = (h % 2) * 128
                                inc(v.tensor_copy(t_xtok[:, slm:slm + 128],
                                                  pXt[:, :]), "ve")
                            sect("vector", f_v5)
                            ve_xtok = C["ve"]

                            if h % 2 == 0:
                                def f_t6(t, w=pv_wb_free, bcols=bcols):
                                    t.wait_ge(sems["ve"], w)
                                    inc(t.transpose(pBt[:, :], A1[:, bcols],
                                                    idbf[:, :]), "pe")
                                sect("tensor", f_t6)
                                pe_tb = C["pe"]

                                def f_v6(v, pe_tb=pe_tb, h=h, kq=kq, jj=jj):
                                    v.wait_ge(sems["pe"], pe_tb)
                                    wc = kq * 16 + h
                                    sw = (jj % 2) * 128
                                    v.tensor_scalar(t_wb[:, sw:sw + 64], pBt[:, 0:64],
                                                    w_tok[:, wc:wc + 1], None,
                                                    op0=ALU.mult)
                                    inc(v.tensor_scalar(t_wb[:, sw + 64:sw + 128],
                                                        pBt[:, 64:128],
                                                        w_tok[:, wc + 1:wc + 2], None,
                                                        op0=ALU.mult), "ve")
                                sect("vector", f_v6)
                                ve_wb = C["ve"]

                            def f_t7(t, w=hist_mp[h % 2], bcols=bcols,
                                     ccols=ccols, base=base, h=h):
                                t.wait_ge(sems["ve"], w)
                                inc(t.matmul(pCBs[h % 2][:, :],
                                             A1[base:base + 64, bcols],
                                             A1[base:base + 64, ccols],
                                             start=True, stop=True), "pe")
                            sect("tensor", f_t7)
                            pe_cb = C["pe"]

                            def f_v7(v, pe_cb=pe_cb, ac_e=e_ready, hh=hh, h=h):
                                v.wait_ge(sems["pe"], pe_cb)
                                v.wait_ge(sems["ac"], ac_e)
                                inc(v.tensor_tensor(
                                    t_mp[:, (h % 2) * 128:(h % 2 + 1) * 128],
                                    pCBs[h % 2][:, :],
                                    t_e[:, hh * 128:(hh + 1) * 128],
                                    op=ALU.mult), "ve")
                            sect("vector", f_v7)
                            ve_mp = C["ve"]
                            mp_hist.append(ve_mp)
                            hist_mp[h % 2] = ve_mp

                            def f_v8(v, ccols=ccols, base=base, h=h, jj=jj,
                                     w=sb_ac[-1]):
                                sl2 = (jj % 2) * 128
                                v.wait_ge(sems["ac"], w)
                                v.tensor_tensor(t_cs[base:base + 64, sl2:sl2 + 128],
                                                A1[base:base + 64, ccols],
                                                expSB[base:base + 64,
                                                      h * 128: h * 128 + 128],
                                                op=ALU.mult)
                                inc(v.tensor_copy(t_hbf[base:base + 64, sl2:sl2 + 128],
                                                  Hp[base:base + 64,
                                                     jj * 128:(jj + 1) * 128]), "ve")
                            sect("vector", f_v8)
                            ve_cs = C["ve"]

                            def f_t8(t, wb=ve_wb, xt=ve_xtok, hu=pv_hup,
                                     base=base, h=h, jj=jj):
                                sw = (jj % 2) * 128
                                slm = (h % 2) * 128
                                t.wait_ge(sems["ve"], wb)
                                t.wait_ge(sems["ve"], xt)
                                t.wait_ge(sems["ve"], hu)
                                inc(t.matmul(pdH[base:base + 64, :],
                                             t_wb[:, sw + base:sw + base + 64],
                                             t_xtok[:, slm:slm + 128],
                                             start=True, stop=True,
                                             tile_position=(0, base)), "pe")
                            sect("tensor", f_t8)
                            pe_dh = C["pe"]

                            def f_t9(t, mp=ve_mp, cs=ve_cs, yg=hist_yg[h % 2],
                                     base=base, h=h, jj=jj):
                                sl2 = (jj % 2) * 128
                                slm = (h % 2) * 128
                                t.wait_ge(sems["ve"], mp)
                                t.wait_ge(sems["ve"], cs)
                                t.wait_ge(sems["ve"], yg)
                                t.matmul(pYs[h % 2][:, :],
                                         t_xtok[:, slm:slm + 128],
                                         t_mp[:, slm:slm + 128],
                                         start=True, stop=False)
                                inc(t.matmul(pYs[h % 2][:, :],
                                             t_hbf[base:base + 64, sl2:sl2 + 128],
                                             t_cs[base:base + 64, sl2:sl2 + 128],
                                             start=False, stop=True), "pe")
                            sect("tensor", f_t9)
                            pe_y = C["pe"]

                            def f_v9(v, pe_dh=pe_dh, pe_y=pe_y, base=base, h=h,
                                     jj=jj, gcols=gcols, wal=sb_ac[-1]):
                                v.wait_ge(sems["pe"], pe_dh)
                                v.wait_ge(sems["ac"], wal)
                                v.drain()
                                alpha = alph[base:base + 64, h:h + 1]
                                inc(v.scalar_tensor_tensor(
                                    Hp[base:base + 64, jj * 128:(jj + 1) * 128],
                                    Hp[base:base + 64, jj * 128:(jj + 1) * 128],
                                    alpha, pdH[base:base + 64, :],
                                    op0=ALU.mult, op1=ALU.add), "ve")
                                v.wait_ge(sems["pe"], pe_y)
                                v.drain()
                                ins = v.tensor_tensor(A1[:, gcols], pYs[h % 2][:, :],
                                                      A1[:, gcols], op=ALU.mult)
                                inc(ins, "ve")
                                note("vector", ins)
                            sect("vector", f_v9)
                            pv_hup = C["ve"] - 1
                            hist_yg[h % 2] = C["ve"]
                            pv_xtok = ve_xtok
                            pv_wb_free = ve_wb
                    e_ac_prev_chunk = sb_ac[-1]
                    prev_mp_last = mp_hist[-1]
                    prev_pe_sb = pe_sb[-1]
                    state["ssd_prev_yg"] = max(hist_yg)
                state["ssd_prev_ac"] = C["ac"]
                barrier()

        def rms_wout_phase(half, q):
            with (pt("p_s1", [1, 512], F32) as p_s1,
                  pt("p_rb", [128, 512], F32) as p_rb,
                  pt("p_w0", [128, 512], F32) as p_w0,
                  pt("p_w1", [128, 512], F32) as p_w1):
                p_w = [p_w0, p_w1]
                ts0 = q * QT
                pe_rms = []
                for h in range(NIC):
                    def f_v1(v, h=h):
                        if h > 0:
                            v.wait_ge(sems["pe"], pe_rms[h - 1])
                        gsl = A1[:, (16 + h) * QT:(17 + h) * QT]
                        inc(v.tensor_tensor(ysq_t[:, :], gsl, gsl, op=ALU.mult), "ve")
                    sect("vector", f_v1)
                    ve_y = C["ve"]

                    def f_t1(t, h=h, ve_y=ve_y):
                        t.wait_ge(sems["ve"], ve_y)
                        ins = t.matmul(p_s1[:, :], onescol[:, :], ysq_t[:, :],
                                       start=(h == 0), stop=(h == NIC - 1))
                        pe_rms.append(inc(ins, "pe"))
                    sect("tensor", f_t1)

                def f_s(s):
                    s.wait_ge(sems["pe"], pe_rms[-1])
                    inc(s.activation(rms_t[:, :], p_s1[:, :], AF.Sqrt,
                                     bias=cc[0:1, 2:3], scale=1.0 / DI), "ac")
                sect("scalar", f_s)
                ac_r = C["ac"]

                def f_v2(v):
                    v.wait_ge(sems["ac"], ac_r)
                    inc(v.reciprocal(rms_t[:, :], rms_t[:, :]), "ve")
                sect("vector", f_v2)
                ve_rr = C["ve"]

                def f_t2(t):
                    t.wait_ge(sems["ve"], ve_rr)
                    inc(t.matmul(p_rb[:, :], onesrow[:, :], rms_t[0:1, :],
                                 start=True, stop=True), "pe")
                sect("tensor", f_t2)
                pe_rb2 = C["pe"]

                yn_ve = []

                def f_v3(v):
                    v.wait_ge(sems["pe"], pe_rb2)
                    for h in range(NIC):
                        gsl = A1[:, (16 + h) * QT:(17 + h) * QT]
                        ins = v.scalar_tensor_tensor(gsl, gsl, normw[:, h:h + 1],
                                                     p_rb[:, :], op0=ALU.mult,
                                                     op1=ALU.mult)
                        yn_ve.append(inc(ins, "ve"))
                        v.drain()
                sect("vector", f_v3)

                wo_dma, wo_cast, wo_mm, wo_ev = [], [], [], []
                for o in range(NKC):
                    p = o % 2

                    def f_q(g, o=o, p=p):
                        if o >= 2:
                            g.wait_ge(sems["ve"], wo_cast[o - 2])
                        else:
                            g.wait_ge(sems[state["stg_cast_last"][0]], state["stg_cast_last"][1])
                        ins = g.dma_start(out=stg[:, p * 2048:(p + 1) * 2048],
                                          in_=wout[o, :, :])
                        wo_dma.append(inc(ins, f"dW{p}", 16))
                    sect("sync", f_q)

                    def f_c(v, o=o, p=p):
                        v.wait_ge(sems[f"dW{p}"], wo_dma[o])
                        if o >= 2:
                            v.wait_ge(sems["pe"], wo_mm[o - 2])
                        ins = v.tensor_copy(wbf[:, p * 2048:(p + 1) * 2048],
                                            stg[:, p * 2048:(p + 1) * 2048])
                        wo_cast.append(inc(ins, "ve"))
                    sect("vector", f_c)

                    def f_t(t, o=o, p=p):
                        t.wait_ge(sems["ve"], wo_cast[o])
                        t.wait_ge(sems["ve"], yn_ve[-1])
                        if o >= 2:
                            t.wait_ge(sems["ve"], wo_ev[o - 2])
                        ins = None
                        for kk in range(NIC):
                            ins = t.matmul(p_w[p][:, :],
                                           wbf[:, p * 2048 + kk * 128:
                                               p * 2048 + (kk + 1) * 128],
                                           A1[:, (16 + kk) * QT:(17 + kk) * QT],
                                           start=(kk == 0), stop=(kk == NIC - 1))
                        wo_mm.append(inc(ins, "pe"))
                    sect("tensor", f_t)

                    def f_v4(v, o=o, p=p):
                        v.wait_ge(sems["pe"], wo_mm[o])
                        asl = acc[:, o * HT + ts0: o * HT + ts0 + QT]
                        ins = v.tensor_tensor(asl, p_w[p][:, :], asl, op=ALU.add)
                        wo_ev.append(inc(ins, "ve"))
                        v.drain()
                        note("vector", ins)
                    sect("vector", f_v4)
                state["stg_cast_last"] = ("ve", wo_cast[-1])
                barrier()

        def router_phase(half):
            with (pt("p_lg", [8, 1024], F32) as p_lg,
                  pt("p_tok0", [128, 8], F32) as p_tok0,
                  pt("p_tok1", [128, 8], F32) as p_tok1,
                  pt("p_st0", [8, 128], F32) as p_st0,
                  pt("p_st1", [8, 128], F32) as p_st1,
                  pt("p_sb", [128, 512], F32) as p_sb,
                  pt("p_sb2", [128, 512], F32) as p_sb2):
                p_tok = [p_tok0, p_tok1]
                p_st = [p_st0, p_st1]

                def f_t(t):
                    ins = None
                    for tsl in range(2):
                        for k in range(NKC):
                            ins = t.matmul(p_lg[:, tsl * 512:(tsl + 1) * 512],
                                           wrout_bf[:, k * 8:(k + 1) * 8],
                                           xn[:, k * HT + tsl * 512:
                                              k * HT + (tsl + 1) * 512],
                                           start=(k == 0), stop=(k == NKC - 1))
                    inc(ins, "pe")
                sect("tensor", f_t)
                pe_lgv = C["pe"]

                def f_v(v):
                    v.wait_ge(sems["pe"], pe_lgv)
                    inc(v.tensor_scalar(lt_r[:, :], p_lg[:, :], routb[:, :], None,
                                        op0=ALU.add), "ve")
                sect("vector", f_v)
                ve_lt = C["ve"]

                exp_ac, st_ve = [], []
                for tc in range(HT // 128):
                    p = tc % 2

                    def f_t1(t, tc=tc, p=p):
                        t.wait_ge(sems["ve"], ve_lt)
                        if tc >= 2:
                            t.wait_ge(sems["ac"], exp_ac[tc - 2])
                        inc(t.transpose(p_tok[p][:, :],
                                        lt_r[:, tc * 128:(tc + 1) * 128],
                                        idf[0:8, 0:8]), "pe")
                    sect("tensor", f_t1)
                    pe_tt = C["pe"]

                    def f_v1(v, pe_tt=pe_tt, p=p):
                        v.wait_ge(sems["pe"], pe_tt)
                        inc(v.tensor_reduce(r_sc[:, 0:1], p_tok[p][:, :], axis=AXX,
                                            op=ALU.max, negate=True), "ve")
                    sect("vector", f_v1)
                    ve_nm = C["ve"]

                    def f_s1(s, ve_nm=ve_nm, p=p):
                        s.wait_ge(sems["ve"], ve_nm)
                        ins = s.activation(e_t[:, :], p_tok[p][:, :], AF.Exp,
                                           bias=r_sc[:, 0:1], scale=1.0)
                        exp_ac.append(inc(ins, "ac"))
                    sect("scalar", f_s1)
                    ac_ex = C["ac"]

                    def f_v2(v, ac_ex=ac_ex):
                        v.wait_ge(sems["ac"], ac_ex)
                        v.tensor_reduce(r_sc[:, 1:2], e_t[:, :], axis=AXX, op=ALU.add)
                        v.drain()
                        v.reciprocal(r_sc[:, 2:3], r_sc[:, 1:2])
                        v.tensor_scalar(mask_t[:, :], e_t[:, :], 1.0, None,
                                        op0=ALU.is_equal)
                        v.drain()
                        inc(v.tensor_scalar(mask_t[:, :], mask_t[:, :], r_sc[:, 2:3],
                                            None, op0=ALU.mult), "ve")
                    sect("vector", f_v2)
                    ve_msk = C["ve"]

                    def f_t2(t, ve_msk=ve_msk, tc=tc, p=p):
                        t.wait_ge(sems["ve"], ve_msk)
                        if tc >= 2:
                            t.wait_ge(sems["ve"], st_ve[tc - 2])
                        inc(t.transpose(p_st[p][:, :], mask_t[:, :], idf[:, :]), "pe")
                    sect("tensor", f_t2)
                    pe_st2 = C["pe"]

                    def f_v3(v, pe_st2=pe_st2, tc=tc, p=p):
                        v.wait_ge(sems["pe"], pe_st2)
                        ins = v.tensor_copy(s_T[0:8, tc * 128:(tc + 1) * 128],
                                            p_st[p][:, :])
                        st_ve.append(inc(ins, "ve"))
                    sect("vector", f_v3)

                state["s_T_ready"] = st_ve[-1]
                barrier()

        def ffn_phase(half):
            for e in range(NE):
                def f_vb(v, e=e):
                    v.wait_ge(sems["ve"], state["s_T_ready"])
                    if state["shtmp_guard"]:
                        v.wait_ge(sems["dG"], state["shtmp_guard"])
                    ins = v.stream_shuffle(shtmp[:, :], s_T[:, :], [e] * 32)
                    inc(ins, "ve")
                sect("vector", f_vb)
                sh_rdy = C["ve"]

                def f_gb(g, e=e, sh_rdy=sh_rdy):
                    g.wait_ge(sems["ve"], sh_rdy)
                    if state["sb2_guard"][e % 2]:
                        g.wait_ge(sems["ve"], state["sb2_guard"][e % 2])
                    ins = g.dma_start(
                        out=s_b2[0:128:32, (e % 2) * HT:(e % 2 + 1) * HT],
                        in_=shtmp[0:4, :])
                    state["shtmp_guard"] = inc(ins, "dG", 16)
                sect("gpsimd", f_gb)
                dg_b = C["dG"]

                def f_vb2(v, e=e, dg_b=dg_b):
                    v.wait_ge(sems["dG"], dg_b)
                    sl = s_b2[:, (e % 2) * HT:(e % 2 + 1) * HT]
                    ins = v.stream_shuffle(sl, sl, [0] * 32)
                    inc(ins, "ve")
                sect("vector", f_vb2)
                sb2_rdy = C["ve"]
                with (pt("p_fa0", [128, 1024], F32) as p_fa0,
                      pt("p_fg0", [128, 1024], F32) as p_fg0,
                      pt("p_fa1", [128, 1024], F32) as p_fa1,
                      pt("p_fg1", [128, 1024], F32) as p_fg1):
                    p_fa = [p_fa0, p_fa1]
                    p_fg = [p_fg0, p_fg1]
                    pref = state.get("ffn_pref")
                    if pref is not None:
                        w1_dma = list(pref[0])
                        w1_cast = list(pref[1])
                        state["ffn_pref"] = None
                    else:
                        w1_dma, w1_cast = [], []
                    w1_amm, w1_gmm, w1_hg, w1_silu = [], [], [], []

                    def emit_hg(jh):
                        ph = jh % 2

                        def f_v(v, jh=jh, ph=ph):
                            v.wait_ge(sems["ac"], w1_silu[jh])
                            v.wait_ge(sems["pe"], w1_gmm[jh])
                            ins = v.tensor_tensor(
                                A1[:, jh * HT:(jh + 1) * HT],
                                wbf[:, ph * 2048: ph * 2048 + 1024],
                                p_fg[ph][:, :], op=ALU.mult)
                            w1_hg.append(inc(ins, "ve"))
                        sect("vector", f_v)

                    for j in range(32):
                        p = j % 2
                        if j < len(w1_dma):
                            pass
                        else:
                            def f_q(g, j=j, p=p, e=e):
                                if j >= 2:
                                    g.wait_ge(sems["ve"], w1_cast[j - 2])
                                else:
                                    g.wait_ge(sems[state["stg_cast_last"][0]],
                                              state["stg_cast_last"][1])
                                ins = g.dma_start(
                                    out=stg[:, p * 2048:(p + 1) * 2048],
                                    in_=w1t[e, :, j * 2048:(j + 1) * 2048])
                                w1_dma.append(inc(ins, f"dW{p}", 16))
                            sect("sync", f_q)

                            def f_cv(v, j=j, p=p, phg=state["prev_ffn_hg"]):
                                v.wait_ge(sems[f"dW{p}"], w1_dma[j])
                                if j >= 2:
                                    v.wait_ge(sems["ve"], w1_hg[j - 2])
                                elif phg:
                                    v.wait_ge(sems["ve"], phg)
                                ins = v.tensor_copy(wbf[:, p * 2048:(p + 1) * 2048],
                                                    stg[:, p * 2048:(p + 1) * 2048])
                                w1_cast.append(inc(ins, "ve"))
                            sect("vector", f_cv)

                        def f_t(t, j=j, p=p, pev=tuple(state["prev_ffn_ev_pair"])):
                            t.wait_ge(sems["ve"], w1_cast[j])
                            if j >= 2:
                                t.wait_ge(sems["ac"], w1_silu[j - 2])
                                t.wait_ge(sems["ve"], w1_hg[j - 2])
                            elif pev[p]:
                                t.wait_ge(sems["ve"], pev[p])
                            ins = None
                            for k in range(NKC):
                                for tsl in range(2):
                                    ins = t.matmul(
                                        p_fa[p][:, tsl * 512:(tsl + 1) * 512],
                                        wbf[:, p * 2048 + k * 128:
                                            p * 2048 + (k + 1) * 128],
                                        xn[:, k * HT + tsl * 512:
                                           k * HT + (tsl + 1) * 512],
                                        start=(k == 0), stop=(k == NKC - 1))
                            w1_amm.append(inc(ins, "pe"))
                            for k in range(NKC):
                                for tsl in range(2):
                                    ins = t.matmul(
                                        p_fg[p][:, tsl * 512:(tsl + 1) * 512],
                                        wbf[:, p * 2048 + 1024 + k * 128:
                                            p * 2048 + 1024 + (k + 1) * 128],
                                        xn[:, k * HT + tsl * 512:
                                           k * HT + (tsl + 1) * 512],
                                        start=(k == 0), stop=(k == NKC - 1))
                            w1_gmm.append(inc(ins, "pe"))
                        sect("tensor", f_t)

                        def f_s2(s, j=j, p=p):
                            s.wait_ge(sems["pe"], w1_amm[j])
                            ins = s.activation(wbf[:, p * 2048: p * 2048 + 1024],
                                               p_fa[p][:, :], AF.Silu)
                            w1_silu.append(inc(ins, "ac"))
                        sect("scalar", f_s2)

                        if j >= 1:
                            emit_hg(j - 1)
                    emit_hg(31)

                with contextlib.ExitStack() as fs:
                    p_fo = [fs.enter_context(
                        pt(f"p_fo{i}", [128, 512], F32)) for i in range(8)]
                    ev_done = []
                    prev_cast = None
                    for og in range(4):
                        gp_par = og % 2
                        w2_dma, w2_cast, w2_mm = [], [], []
                        for j in range(32):
                            s4 = j % 4

                            def f_q(g, j=j, s4=s4, e=e, og=og, pc=prev_cast,
                                    pw2=state["prev_w2_cast"]):
                                if j >= 4:
                                    g.wait_ge(sems["ve"], w2_cast[j - 4])
                                elif og == 0:
                                    if pw2:
                                        g.wait_ge(sems["ve"], pw2)
                                else:
                                    g.wait_ge(sems["ve"], pc[-1])
                                ins = g.dma_start(
                                    out=stgw2[:, s4 * 256:(s4 + 1) * 256],
                                    in_=w2t[e, og, :, j * 256:(j + 1) * 256])
                                w2_dma.append(inc(ins, f"dW{2 + s4}", 16))
                            sect("sync", f_q)

                            def f_c(v, j=j, s4=s4, og=og,
                                    pmm=state["prev_ffn_mm"]):
                                v.wait_ge(sems[f"dW{2 + s4}"], w2_dma[j])
                                if j >= 4:
                                    v.wait_ge(sems["pe"], w2_mm[j - 4])
                                elif og == 0:
                                    v.wait_ge(sems["pe"], pmm)
                                ins = v.tensor_copy(
                                    w2bf[:, s4 * 256:(s4 + 1) * 256],
                                    stgw2[:, s4 * 256:(s4 + 1) * 256])
                                w2_cast.append(inc(ins, "ve"))
                            sect("vector", f_c)

                            def f_t(t, j=j, s4=s4, og=og, gp_par=gp_par):
                                t.wait_ge(sems["ve"], w2_cast[j])
                                if og == 0:
                                    t.wait_ge(sems["ve"], w1_hg[j])
                                if j == 0:
                                    if og >= 2:
                                        t.wait_ge(sems["ve"], ev_done[og - 2])
                                    else:
                                        t.wait_ge(sems["ac"], w1_silu[-1] if gp_par
                                                  else w1_silu[-2])
                                        t.wait_ge(sems["ve"], w1_hg[-1] if gp_par
                                                  else w1_hg[-2])
                                ins = None
                                for o2 in range(2):
                                    for tsl in range(2):
                                        ins = t.matmul(
                                            p_fo[gp_par * 4 + o2 * 2 + tsl][:, :],
                                            w2bf[:, s4 * 256 + o2 * 128:
                                                 s4 * 256 + (o2 + 1) * 128],
                                            A1[:, j * HT + tsl * 512:
                                               j * HT + (tsl + 1) * 512],
                                            start=(j == 0), stop=(j == 31))
                                w2_mm.append(inc(ins, "pe"))
                            sect("tensor", f_t)
                        prev_cast = w2_cast
                        if og == 2 and e < NE - 1:
                            pf_dma, pf_cast = [], []
                            for jp in range(2):
                                def f_qp(g, jp=jp, e=e):
                                    g.wait_ge(sems["ve"], w1_cast[30 + jp])
                                    ins = g.dma_start(
                                        out=stg[:, jp * 2048:(jp + 1) * 2048],
                                        in_=w1t[e + 1, :, jp * 2048:(jp + 1) * 2048])
                                    pf_dma.append(inc(ins, f"dW{jp}", 16))
                                sect("sync", f_qp)

                                def f_cp(v, jp=jp):
                                    v.wait_ge(sems[f"dW{jp}"], pf_dma[jp])
                                    v.wait_ge(sems["ve"], w1_hg[-1])
                                    ins = v.tensor_copy(
                                        wbf[:, jp * 2048:(jp + 1) * 2048],
                                        stg[:, jp * 2048:(jp + 1) * 2048])
                                    pf_cast.append(inc(ins, "ve"))
                                sect("vector", f_cp)
                            state["ffn_pref"] = (pf_dma, pf_cast)

                        def f_v2(v, e=e, og=og, gp_par=gp_par, wlast=w2_mm[-1],
                                 sb2_rdy=sb2_rdy):
                            v.wait_ge(sems["pe"], wlast)
                            v.wait_ge(sems["ve"], sb2_rdy)
                            ins = None
                            for o2 in range(2):
                                o = og * 2 + o2
                                for tsl in range(2):
                                    asl = acc[:, o * HT + tsl * 512:
                                              o * HT + (tsl + 1) * 512]
                                    ssl = s_b2[:, (e % 2) * HT + tsl * 512:
                                               (e % 2) * HT + (tsl + 1) * 512]
                                    v.tensor_tensor(tmp1024[:, 0:512],
                                                    p_fo[gp_par * 4 + o2 * 2 + tsl][:, :],
                                                    ssl, op=ALU.mult)
                                    v.drain()
                                    ins = v.tensor_tensor(asl, tmp1024[:, 0:512],
                                                          asl, op=ALU.add)
                                    v.drain()
                            ev_done.append(inc(ins, "ve"))
                            note("vector", ins)
                        sect("vector", f_v2)
                        state["prev_ffn_mm"] = w2_mm[-1]
                    state["prev_ffn_ev"] = ev_done[-1]
                    state["prev_ffn_ev_pair"] = [ev_done[-2], ev_done[-1]]
                    state["prev_w2_cast"] = prev_cast[-1]
                    state["sb2_guard"][e % 2] = ev_done[-1]
                    state["stg_cast_last"] = ("ve", w1_cast[-1])
                    state["prev_ffn_hg"] = w1_hg[-1]
            barrier()

        def store_phase(half):
            tgt = {s: C[s] for s in ("pe", "ve", "ac", "gp")}

            def f_g(g):
                for s in ("pe", "ve", "ac", "gp"):
                    if tgt[s] > 0:
                        g.wait_ge(sems[s], tgt[s])
                for cch in range(NKC):
                    inc(g.dma_start(
                        out=out_t[cch * 128:(cch + 1) * 128,
                                  half * HT:(half + 1) * HT],
                        in_=acc[:, cch * HT:(cch + 1) * HT]), "dS", 16)
            sect("gpsimd", f_g)

        init_phase()
        for half in range(n_half):
            for q in range(2):
                ln_phase(half, q)
                z_phase(half, q)
                gate_phase(half, q)
                ssd_phase(half, q)
                rms_wout_phase(half, q)
            router_phase(half)
            ffn_phase(half)
            store_phase(half)
        ds_t = C["dS"]
        sect("gpsimd", lambda g: g.wait_ge(sems["dS"], ds_t))

    return nc


def host_prep(inputs, n_half=2):
    LT = n_half * HT
    W_in = np.asarray(inputs["L0_attn_W_in"], dtype=np.float32)
    conv_w = np.asarray(inputs["L0_attn_conv_w"], dtype=np.float32)
    conv_b = np.asarray(inputs["L0_attn_conv_b"], dtype=np.float32)
    dp = np.asarray(inputs["L0_attn_delta_param"], dtype=np.float32)
    W_gate = np.asarray(inputs["L0_attn_W_gate"], dtype=np.float32)
    norm_w = np.asarray(inputs["L0_attn_norm_w"], dtype=np.float32)
    W_out = np.asarray(inputs["L0_attn_W_out"], dtype=np.float32)
    r_w = np.asarray(inputs["L0_ffn_router_w"], dtype=np.float32)
    r_b = np.asarray(inputs["L0_ffn_router_b"], dtype=np.float32)
    w1 = np.asarray(inputs["L0_ffn_w1"], dtype=np.float32)
    w2 = np.asarray(inputs["L0_ffn_w2"], dtype=np.float32)

    ctg = np.ascontiguousarray
    shared = {}
    shared["win_x"] = ctg(W_in[:, :2048].reshape(8, 128, 16, 128)
                          .transpose(2, 1, 0, 3).reshape(16, 128, 1024))
    shared["win_b"] = ctg(W_in[:, 2048:3072].reshape(8, 128, 16, 64)
                          .transpose(2, 1, 0, 3).reshape(16, 128, 512))
    shared["win_c"] = ctg(W_in[:, 3072:4096].reshape(8, 128, 16, 64)
                          .transpose(2, 1, 0, 3).reshape(16, 128, 512))
    shared["win_dt"] = ctg(W_in[:, 4096:4112].reshape(8, 128, 16)
                           .transpose(1, 0, 2).reshape(128, 128))
    shared["wgate"] = ctg(W_gate.reshape(8, 128, 16, 128)
                          .transpose(2, 1, 0, 3).reshape(16, 128, 1024))
    shared["wout"] = ctg(W_out.reshape(16, 128, 8, 128)
                         .transpose(2, 1, 0, 3).reshape(8, 128, 2048))
    shared["wrout"] = ctg(r_w.reshape(8, 128, 8).transpose(1, 0, 2).reshape(128, 64))
    order = np.empty(64, dtype=np.int64)
    order[0::2] = np.arange(32)
    order[1::2] = np.arange(32, 64)
    w1r = w1.reshape(8, 8, 128, 64, 128).transpose(0, 2, 3, 1, 4)
    shared["w1t"] = ctg(w1r[:, :, order].reshape(8, 128, 65536))
    w2r = w2.reshape(8, 32, 128, 4, 2, 128).transpose(0, 3, 2, 1, 4, 5)
    shared["w2t"] = ctg(w2r.reshape(8, 4, 128, 8192))
    ident = np.eye(128, dtype=np.float32)
    shared["c_idbf"] = ident.astype(BF)
    shared["c_idf"] = ident
    sidx = np.arange(128)
    shared["c_tri01"] = (sidx[:, None] <= sidx[None, :]).astype(np.float32)
    shared["c_trimask"] = (NEG_BIG * (sidx[:, None] > sidx[None, :])).astype(np.float32)
    shared["c_onesrow"] = np.ones((1, 128), dtype=np.float32)
    shared["c_onescol"] = np.ones((128, 1), dtype=np.float32).astype(BF)
    cc = np.zeros((128, 4), dtype=np.float32)
    cc[:, 0] = 1.0
    cc[:, 1] = LN_EPS
    cc[:, 2] = RMS_EPS
    shared["c_cc"] = cc
    shared["c_normw"] = ctg(norm_w.reshape(16, 128).T)
    shared["c_convb"] = ctg(conv_b.reshape(16, 128).T)
    shared["c_convw"] = ctg(conv_w.reshape(16, 128, 4).transpose(1, 0, 2)
                            .reshape(128, 64))
    shared["c_dptile"] = np.tile(dp, (128, 4)).astype(np.float32)
    shared["c_routb"] = r_b.reshape(8, 1)
    return shared


def kernel(**inputs):
    n_half = 2
    u = np.asarray(inputs["L0_attn_u"], dtype=np.float32)
    ncores = u.shape[0]
    shared = host_prep(inputs, n_half)
    nc = build_program(n_half)
    in_maps = []
    for c in range(ncores):
        m = dict(shared)
        m["u_t"] = np.ascontiguousarray(u[c].T)
        in_maps.append(m)
    res = run_bass_kernel_spmd(nc, in_maps, core_ids=list(range(ncores)))
    out = np.stack([np.ascontiguousarray(r["out_t"].T) for r in res.results], axis=0)
    return out


# version 10
# speedup vs baseline: 1.0065x; 1.0065x over previous
import numpy as np
import ml_dtypes
import contextlib
import concourse.bass as bass
import concourse.mybir as mybir
from concourse.bass_utils import run_bass_kernel_spmd

F32 = mybir.dt.float32
BF16 = mybir.dt.bfloat16
AF = mybir.ActivationFunctionType
ALU = mybir.AluOpType
BF = ml_dtypes.bfloat16
AXX = mybir.AxisListType.X

D = 1024
DI = 2048
NH = 16
DC = 4
Q = 128
QT = 512
HT = 1024
NKC = 8
NIC = 16
NE = 8
LN_EPS = 1e-5
RMS_EPS = 1.1920928955078125e-07
NEG_BIG = 1e30


def build_program(n_half=2):
    nc = bass.Bass("TRN2", target_bir_lowering=False, debug=False)
    LT = n_half * HT

    u_t = nc.dram_tensor("u_t", [D, LT], F32, kind="ExternalInput").ap()
    out_t = nc.dram_tensor("out_t", [D, LT], F32, kind="ExternalOutput").ap()
    win_x = nc.dram_tensor("win_x", [NIC, 128, 1024], F32, kind="ExternalInput").ap()
    win_b = nc.dram_tensor("win_b", [NH, 128, 512], F32, kind="ExternalInput").ap()
    win_c = nc.dram_tensor("win_c", [NH, 128, 512], F32, kind="ExternalInput").ap()
    win_dt = nc.dram_tensor("win_dt", [128, 128], F32, kind="ExternalInput").ap()
    wgate = nc.dram_tensor("wgate", [NIC, 128, 1024], F32, kind="ExternalInput").ap()
    wout = nc.dram_tensor("wout", [NKC, 128, 2048], F32, kind="ExternalInput").ap()
    wrout = nc.dram_tensor("wrout", [128, 64], F32, kind="ExternalInput").ap()
    w1t = nc.dram_tensor("w1t", [NE, 128, 65536], F32, kind="ExternalInput").ap()
    w2t = nc.dram_tensor("w2t", [NE, 4, 128, 8192], F32, kind="ExternalInput").ap()
    c_idbf = nc.dram_tensor("c_idbf", [128, 128], BF16, kind="ExternalInput").ap()
    c_idf = nc.dram_tensor("c_idf", [128, 128], F32, kind="ExternalInput").ap()
    c_tri01 = nc.dram_tensor("c_tri01", [128, 128], F32, kind="ExternalInput").ap()
    c_trimask = nc.dram_tensor("c_trimask", [128, 128], F32, kind="ExternalInput").ap()
    c_onesrow = nc.dram_tensor("c_onesrow", [1, 128], F32, kind="ExternalInput").ap()
    c_onescol = nc.dram_tensor("c_onescol", [128, 1], BF16, kind="ExternalInput").ap()
    c_cc = nc.dram_tensor("c_cc", [128, 4], F32, kind="ExternalInput").ap()
    c_normw = nc.dram_tensor("c_normw", [128, 16], F32, kind="ExternalInput").ap()
    c_convb = nc.dram_tensor("c_convb", [128, 16], F32, kind="ExternalInput").ap()
    c_convw = nc.dram_tensor("c_convw", [128, 64], F32, kind="ExternalInput").ap()
    c_dptile = nc.dram_tensor("c_dptile", [128, 64], F32, kind="ExternalInput").ap()
    c_routb = nc.dram_tensor("c_routb", [8, 1], F32, kind="ExternalInput").ap()

    es = contextlib.ExitStack()
    with es:
        sb = lambda name, shape, dt: es.enter_context(nc.sbuf_tensor(name, shape, dt))
        acc = sb("acc", [128, NKC * HT], F32)
        xn = sb("xn", [128, NKC * HT], BF16)
        A1 = sb("A1", [128, 32 * HT], BF16)
        stg = sb("stg", [128, 2 * 2048], F32)
        wbf = sb("wbf", [128, 2 * 2048], BF16)
        stgw2 = sb("stgw2", [128, 1024], F32)
        w2bf = sb("w2bf", [128, 1024], BF16)
        x_pre = sb("x_pre", [128, QT + 3], F32)
        tails = sb("tails", [128, NIC * 3], F32)
        SBm = sb("SBm", [128, NH * Q], F32)
        expSB = sb("expSB", [128, NH * Q], BF16)
        alph = sb("alph", [128, 16], F32)
        dtw = sb("dtw", [128, 4 * 64], F32)
        nsc_tok = sb("nsc_tok", [128, 64], F32)
        nsc_hm = sb("nsc_hm", [64, 128], F32)
        w_hm = sb("w_hm", [64, 128], F32)
        w_tok = sb("w_tok", [128, 64], F32)
        nb16 = sb("nb16", [64, 1], F32)
        Hp = sb("Hp", [128, 8 * 128], F32)
        t_xtok = sb("t_xtok", [128, 256], BF16)
        t_wb = sb("t_wb", [128, 256], BF16)
        t_mp = sb("t_mp", [128, 256], BF16)
        t_cs = sb("t_cs", [128, 256], BF16)
        t_e = sb("t_e", [128, 512], F32)
        t_hbf = sb("t_hbf", [128, 256], BF16)
        rowbuf = sb("rowbuf", [1, 512], F32)
        s_T = sb("s_T", [32, HT], BF16)
        s_b2 = sb("s_b2", [128, 2 * HT], BF16)
        shtmp = sb("shtmp", [32, HT], BF16)
        ln_t = sb("ln_t", [1, 3 * QT], F32)
        rms_t = sb("rms_t", [1, QT], F32)
        tmp1024 = sb("tmp1024", [128, 1024], F32)
        ubf = sb("ubf", [128, 2 * 512], BF16)
        usq = sb("usq", [128, 2 * 512], BF16)
        ysq_t = sb("ysq_t", [128, 512], BF16)
        lt_r = sb("lt_r", [8, HT], F32)
        e_t = sb("e_t", [128, 8], F32)
        mask_t = sb("mask_t", [128, 8], F32)
        r_sc = sb("r_sc", [128, 4], F32)
        idbf = sb("idbf", [128, 128], BF16)
        idf = sb("idf", [128, 128], F32)
        tri01 = sb("tri01", [128, 128], F32)
        trimask = sb("trimask", [128, 128], F32)
        onesrow = sb("onesrow", [1, 128], F32)
        onescol = sb("onescol", [128, 1], BF16)
        cc = sb("cc", [128, 4], F32)
        normw = sb("normw", [128, 16], F32)
        convb = sb("convb", [128, 16], F32)
        convw = sb("convw", [128, 64], F32)
        dptile = sb("dptile", [128, 64], F32)
        routb = sb("routb", [8, 1], F32)
        wrout_s = sb("wrout_s", [128, 64], F32)
        wrout_bf = sb("wrout_bf", [128, 64], BF16)
        wdt_bf = sb("wdt_bf", [128, 128], BF16)

        sems = {}
        for name in ("dW0", "dW1", "dW2", "dW3", "dW4", "dW5", "dG", "dS", "pe", "ve", "ac", "gp"):
            sems[name] = es.enter_context(nc.semaphore(name))
        block = es.enter_context(nc.Block())

        C = {k: 0 for k in sems}
        _updated = set()
        _updated_refs = []
        OWN = {"tensor": "pe", "scalar": "ac", "vector": "ve", "gpsimd": "gp"}

        def inc(ins, s, amt=1):
            assert id(ins) not in _updated, "instruction already carries an update"
            _updated.add(id(ins))
            _updated_refs.append(ins)
            ins.then_inc(sems[s], amt)
            C[s] += amt
            return C[s]

        def sect(engname, fn):
            deco = getattr(block, engname)

            @deco
            def _(eng):
                fn(eng)

        def note(engname, ins):
            if id(ins) not in _updated:
                inc(ins, OWN[engname])
            return ins

        def barrier():
            tgt = {s: C[s] for s in ("pe", "ve", "ac", "gp")}

            def f(e):
                for s in ("pe", "ve", "ac", "gp"):
                    if tgt[s] > 0:
                        e.wait_ge(sems[s], tgt[s])
            for engname in ("tensor", "scalar", "vector", "gpsimd"):
                sect(engname, f)

        state = {"prev_ffn_ev": 0, "prev_ffn_mm": 0, "stg_cast_last": ("ac", 0),
                 "ssd_prev_ac": 0, "ssd_prev_yg": 0, "ssd_prev_d4": 0,
                 "s_T_ready": 0, "sb2_ready": 0, "sb2_guard": [0, 0],
                 "prev_ffn_ev_pair": [0, 0], "prev_w2_cast": 0, "prev_ffn_hg": 0,
                 "shtmp_guard": 0}
        _uid = [0]

        def pt(name, shape, dt):
            _uid[0] += 1
            return nc.psum_tensor(f"{name}_{_uid[0]}", shape, dt)

        def init_phase():
            def f_g(g):
                for dst, src in [(idbf, c_idbf), (idf, c_idf), (tri01, c_tri01),
                                 (trimask, c_trimask), (onesrow, c_onesrow),
                                 (onescol, c_onescol), (cc, c_cc), (normw, c_normw),
                                 (convb, c_convb), (convw, c_convw), (dptile, c_dptile),
                                 (routb, c_routb), (wrout_s, wrout)]:
                    inc(g.dma_start(out=dst[:, :], in_=src[:, :]), "dG", 16)
                inc(g.dma_start(out=stg[:, 0:128], in_=win_dt[:, :]), "dG", 16)
            sect("gpsimd", f_g)
            dg0 = C["dG"]

            def f_s(s):
                s.wait_ge(sems["dG"], dg0)
                s.copy(wdt_bf[:, :], stg[:, 0:128])
                ins = s.copy(wrout_bf[:, :], wrout_s[:, :])
                state["stg_cast_last"] = ("ac", inc(ins, "ac"))
                note("scalar", ins)
            sect("scalar", f_s)

            def f_v(v):
                v.wait_ge(sems["dG"], dg0)
                v.memset(Hp[:, :], 0.0)
                v.memset(s_T[:, :], 0.0)
                v.memset(s_b2[:, :], 0.0)
                note("vector", v.memset(tails[:, :], 0.0))
            sect("vector", f_v)
            barrier()

        def ln_phase(half, q):
            with (pt("p_s1", [1, 512], F32) as p_s1,
                  pt("p_s2", [1, 512], F32) as p_s2,
                  pt("p_rb", [128, 512], F32) as p_rb,
                  pt("p_mrb", [128, 512], F32) as p_mrb):
                ts0 = q * QT
                if q == 0:
                    st_w = C["dS"]
                    dg_before = C["dG"]

                    def f_g(g):
                        g.wait_ge(sems["dS"], st_w)
                        g.wait_ge(sems["dG"], dg_before)
                        for cch in range(NKC):
                            inc(g.dma_start(
                                out=acc[:, cch * HT:(cch + 1) * HT],
                                in_=u_t[cch * 128:(cch + 1) * 128,
                                        half * HT:(half + 1) * HT]), "dG", 16)
                    sect("gpsimd", f_g)
                dg_u = C["dG"]
                pe_base = C["pe"]
                cast_v = []

                def f_v(v):
                    v.wait_ge(sems["dG"], dg_u)
                    for cch in range(NKC):
                        sl = acc[:, cch * HT + ts0: cch * HT + ts0 + QT]
                        p = cch % 2
                        if cch >= 2:
                            v.wait_ge(sems["pe"], pe_base + (cch - 2) + 1)
                        v.tensor_copy(ubf[:, p * 512:(p + 1) * 512], sl)
                        ins = v.tensor_tensor(usq[:, p * 512:(p + 1) * 512], sl, sl,
                                              op=ALU.mult)
                        cast_v.append(inc(ins, "ve"))
                sect("vector", f_v)

                def f_t(t):
                    for cch in range(NKC):
                        p = cch % 2
                        t.wait_ge(sems["ve"], cast_v[cch])
                        t.matmul(p_s1[:, :], onescol[:, :], ubf[:, p * 512:(p + 1) * 512],
                                 start=(cch == 0), stop=(cch == NKC - 1))
                        ins = t.matmul(p_s2[:, :], onescol[:, :],
                                       usq[:, p * 512:(p + 1) * 512],
                                       start=(cch == 0), stop=(cch == NKC - 1))
                        inc(ins, "pe")
                sect("tensor", f_t)
                pe_mm = C["pe"]

                def f_v2(v):
                    v.wait_ge(sems["pe"], pe_mm)
                    m, vv, r = (ln_t[:, i * QT:(i + 1) * QT] for i in range(3))
                    v.tensor_scalar(m, p_s1[:, :], 1.0 / D, None, op0=ALU.mult)
                    v.tensor_scalar(vv, p_s2[:, :], 1.0 / D, None, op0=ALU.mult)
                    v.drain()
                    v.tensor_tensor(r, m, m, op=ALU.mult)
                    v.drain()
                    inc(v.tensor_tensor(vv, vv, r, op=ALU.subtract), "ve")
                sect("vector", f_v2)
                ve_st = C["ve"]

                def f_s(s):
                    s.wait_ge(sems["ve"], ve_st)
                    inc(s.activation(ln_t[:, 2 * QT:], ln_t[:, QT:2 * QT], AF.Sqrt,
                                     bias=cc[0:1, 1:2], scale=1.0), "ac")
                sect("scalar", f_s)
                ac_sq = C["ac"]

                def f_v3(v):
                    v.wait_ge(sems["ac"], ac_sq)
                    r, m = ln_t[:, 2 * QT:], ln_t[:, 0:QT]
                    v.reciprocal(r, r)
                    v.drain()
                    inc(v.tensor_tensor(m, m, r, op=ALU.mult), "ve")
                sect("vector", f_v3)
                ve_r = C["ve"]

                def f_t2(t):
                    t.wait_ge(sems["ve"], ve_r)
                    t.matmul(p_rb[:, :], onesrow[:, :], ln_t[0:1, 2 * QT:],
                             start=True, stop=True)
                    inc(t.matmul(p_mrb[:, :], onesrow[:, :], ln_t[0:1, 0:QT],
                                 start=True, stop=True), "pe")
                sect("tensor", f_t2)
                pe_bc = C["pe"]

                def f_v4(v):
                    v.wait_ge(sems["pe"], pe_bc)
                    for cch in range(NKC):
                        sl = acc[:, cch * HT + ts0: cch * HT + ts0 + QT]
                        xsl = xn[:, cch * HT + ts0: cch * HT + ts0 + QT]
                        v.tensor_tensor(tmp1024[:, 0:QT], sl, p_rb[:, :], op=ALU.mult)
                        v.drain()
                        note("vector", v.tensor_tensor(xsl, tmp1024[:, 0:QT],
                                                       p_mrb[:, :], op=ALU.subtract))
                        v.drain()
                sect("vector", f_v4)
                barrier()

        def z_phase(half, q):
            with (pt("p_z0", [128, 512], F32) as p_z0,
                  pt("p_z1", [128, 512], F32) as p_z1,
                  pt("p_dt", [128, 64], F32) as p_dt):
                p_z = [p_z0, p_z1]
                ts0 = q * QT
                xn_sl = lambda k: xn[:, k * HT + ts0: k * HT + ts0 + QT]

                dma_d, cast_a, mm_pe, evac_v, silu_a = [], [], [], [], []
                for m in range(NIC):
                    p = m % 2

                    def f_q(g, m=m, p=p):
                        if m >= 2:
                            g.wait_ge(sems["gp"], cast_a[m - 2])
                        else:
                            g.wait_ge(sems[state["stg_cast_last"][0]], state["stg_cast_last"][1])
                        ins = g.dma_start(out=stg[:, p * 2048: p * 2048 + 1024],
                                          in_=win_x[m, :, :])
                        dma_d.append(inc(ins, f"dW{p}", 16))
                    sect("sync", f_q)

                    def f_c(g, m=m, p=p):
                        g.wait_ge(sems[f"dW{p}"], dma_d[m])
                        if m >= 2:
                            g.wait_ge(sems["pe"], mm_pe[m - 2])
                        ins = g.tensor_copy(wbf[:, p * 2048: p * 2048 + 1024],
                                            stg[:, p * 2048: p * 2048 + 1024])
                        cast_a.append(inc(ins, "gp"))
                    sect("gpsimd", f_c)

                    def f_t(t, m=m, p=p):
                        t.wait_ge(sems["gp"], cast_a[m])
                        if m >= 2:
                            t.wait_ge(sems["ve"], evac_v[m - 2])
                        for k in range(NKC):
                            ins = t.matmul(p_z[p][:, :],
                                           wbf[:, p * 2048 + k * 128:
                                               p * 2048 + (k + 1) * 128],
                                           xn_sl(k), start=(k == 0), stop=(k == NKC - 1))
                        mm_pe.append(inc(ins, "pe"))
                    sect("tensor", f_t)

                    def f_v(v, m=m, p=p):
                        v.wait_ge(sems["pe"], mm_pe[m])
                        if m >= 1:
                            v.wait_ge(sems["ac"], silu_a[m - 1])
                        v.tensor_copy(x_pre[:, 3:3 + QT], p_z[p][:, :])
                        v.drain()
                        v.tensor_copy(x_pre[:, 0:3], tails[:, m * 3:(m + 1) * 3])
                        v.drain()
                        v.tensor_copy(tails[:, m * 3:(m + 1) * 3], x_pre[:, QT:QT + 3])
                        v.drain()
                        v.tensor_scalar(tmp1024[:, 0:QT], x_pre[:, 0:QT],
                                        convw[:, m * 4: m * 4 + 1], None, op0=ALU.mult)
                        ins = None
                        for kk in range(1, DC):
                            v.drain()
                            ins = v.scalar_tensor_tensor(
                                tmp1024[:, 0:QT], x_pre[:, kk:kk + QT],
                                convw[:, m * 4 + kk: m * 4 + kk + 1],
                                tmp1024[:, 0:QT], op0=ALU.mult, op1=ALU.add)
                        evac_v.append(inc(ins, "ve"))
                    sect("vector", f_v)

                    def f_s2(s, m=m):
                        s.wait_ge(sems["ve"], evac_v[m])
                        ins = s.activation(A1[:, 24 * HT + m * QT: 24 * HT + (m + 1) * QT],
                                           tmp1024[:, 0:QT],
                                           AF.Silu, bias=convb[:, m:m + 1])
                        silu_a.append(inc(ins, "ac"))
                        note("scalar", ins)
                    sect("scalar", f_s2)

                bc_dma, bc_cast, bc_mm, bc_ev = [], [], [], []
                for idx in range(NH):
                    isC = idx >= 8
                    j = idx % 8
                    src = win_c if isC else win_b
                    p = idx % 2

                    def f_q(g, idx=idx, j=j, src=src, p=p):
                        if idx >= 2:
                            g.wait_ge(sems["ve"], bc_cast[idx - 2])
                        else:
                            g.wait_ge(sems["gp"], cast_a[-1])
                        g.dma_start(out=stg[:, p * 2048: p * 2048 + 512],
                                    in_=src[2 * j, :, :]).then_inc(sems[f"dW{p}"], 16)
                        C[f"dW{p}"] += 16
                        ins = g.dma_start(out=stg[:, p * 2048 + 512: p * 2048 + 1024],
                                          in_=src[2 * j + 1, :, :])
                        bc_dma.append(inc(ins, f"dW{p}", 16))
                    sect("sync", f_q)

                    def f_c(v, idx=idx, p=p):
                        v.wait_ge(sems[f"dW{p}"], bc_dma[idx])
                        if idx >= 2:
                            v.wait_ge(sems["pe"], bc_mm[idx - 2])
                        else:
                            v.wait_ge(sems["pe"], mm_pe[-1])
                        ins = v.tensor_copy(wbf[:, p * 2048: p * 2048 + 1024],
                                            stg[:, p * 2048: p * 2048 + 1024])
                        bc_cast.append(inc(ins, "ve"))
                    sect("vector", f_c)

                    def f_t(t, idx=idx, p=p):
                        t.wait_ge(sems["ve"], bc_cast[idx])
                        if idx >= 2:
                            t.wait_ge(sems["ve"], bc_ev[idx - 2])
                        else:
                            t.wait_ge(sems["ve"], evac_v[-1])
                        ins = None
                        for hh in range(2):
                            for k in range(NKC):
                                ins = t.matmul(
                                    p_z[p][hh * 64:(hh + 1) * 64, :],
                                    wbf[:, p * 2048 + hh * 512 + k * 64:
                                        p * 2048 + hh * 512 + (k + 1) * 64],
                                    xn_sl(k), start=(k == 0), stop=(k == NKC - 1),
                                    tile_position=(0, hh * 64))
                        bc_mm.append(inc(ins, "pe"))
                    sect("tensor", f_t)

                    def f_v(v, idx=idx, j=j, isC=isC, p=p):
                        v.wait_ge(sems["pe"], bc_mm[idx])
                        base = (8 * QT if isC else 0) + j * QT
                        ins = v.tensor_copy(A1[:, base: base + QT], p_z[p][:, :])
                        bc_ev.append(inc(ins, "ve"))
                    sect("vector", f_v)

                dt_pe, dt_ve = [], []
                for tc in range(QT // 128):
                    def f_t2(t, tc=tc):
                        if tc >= 1:
                            t.wait_ge(sems["ve"], dt_ve[tc - 1])
                        ins = None
                        for k in range(NKC):
                            ins = t.matmul(
                                p_dt[:, 0:16],
                                xn[:, k * HT + ts0 + tc * 128:
                                   k * HT + ts0 + (tc + 1) * 128],
                                wdt_bf[:, k * 16:(k + 1) * 16],
                                start=(k == 0), stop=(k == NKC - 1))
                        dt_pe.append(inc(ins, "pe"))
                    sect("tensor", f_t2)

                    def f_v2a(v, tc=tc):
                        v.wait_ge(sems["pe"], dt_pe[tc])
                        ins = v.tensor_copy(dtw[:, tc * 16:(tc + 1) * 16], p_dt[:, 0:16])
                        dt_ve.append(inc(ins, "ve"))
                    sect("vector", f_v2a)

                sl_ac = silu_a[-1]

                def f_v2(v):
                    v.wait_ge(sems["ve"], dt_ve[-1])
                    v.wait_ge(sems["ac"], sl_ac)
                    d0, d1 = dtw[:, 0:64], dtw[:, 64:128]
                    t_mx = tmp1024[:, 64:128]
                    v.drain()
                    v.tensor_tensor(d1, d0, dptile[:, :], op=ALU.add)
                    v.drain()
                    inc(v.tensor_scalar(t_mx, d1, 0.0, None, op0=ALU.max), "ve")
                sect("vector", f_v2)
                ve_dt = C["ve"]

                def f_s3(s):
                    s.wait_ge(sems["ve"], ve_dt)
                    d0, d1, d2 = (dtw[:, i * 64:(i + 1) * 64] for i in range(3))
                    t_ax = tmp1024[:, 0:64]
                    s.activation(d2, d0, AF.Abs)
                    s.activation(t_ax, d1, AF.Abs)
                    s.drain()
                    s.activation(t_ax, t_ax, AF.Exp, scale=-1.0,
                                 bias=cc[0:128, 3:4])
                    s.drain()
                    inc(s.activation(t_ax, t_ax, AF.Ln, bias=cc[0:128, 0:1]), "ac")
                sect("scalar", f_s3)
                ac_sp = C["ac"]

                def f_v3(v):
                    v.wait_ge(sems["ac"], ac_sp)
                    d1, d2, d3 = (dtw[:, i * 64:(i + 1) * 64] for i in range(1, 4))
                    t_ax = tmp1024[:, 0:64]
                    t_mx = tmp1024[:, 64:128]
                    v.tensor_tensor(d1, t_ax, t_mx, op=ALU.add)
                    v.drain()
                    note("vector", v.tensor_tensor(d3, d1, d2, op=ALU.mult))
                sect("vector", f_v3)
                state["stg_cast_last"] = ("ve", bc_cast[-1])
                barrier()

        def gate_phase(half, q):
            with (pt("p_z0", [128, 512], F32) as p_z0,
                  pt("p_z1", [128, 512], F32) as p_z1):
                p_z = [p_z0, p_z1]
                ts0 = q * QT
                xn_sl = lambda k: xn[:, k * HT + ts0: k * HT + ts0 + QT]
                g_dma, g_cast, g_mm, g_ev = [], [], [], []
                for m in range(NIC):
                    p = m % 2

                    def f_q(g, m=m, p=p):
                        if m >= 2:
                            g.wait_ge(sems["ve"], g_cast[m - 2])
                        else:
                            g.wait_ge(sems[state["stg_cast_last"][0]], state["stg_cast_last"][1])
                        ins = g.dma_start(out=stg[:, p * 2048: p * 2048 + 1024],
                                          in_=wgate[m, :, :])
                        g_dma.append(inc(ins, f"dW{p}", 16))
                    sect("sync", f_q)

                    def f_c(v, m=m, p=p):
                        v.wait_ge(sems[f"dW{p}"], g_dma[m])
                        if m >= 2:
                            v.wait_ge(sems["pe"], g_mm[m - 2])
                        ins = v.tensor_copy(wbf[:, p * 2048: p * 2048 + 1024],
                                            stg[:, p * 2048: p * 2048 + 1024])
                        g_cast.append(inc(ins, "ve"))
                    sect("vector", f_c)

                    def f_t(t, m=m, p=p):
                        t.wait_ge(sems["ve"], g_cast[m])
                        if m >= 2:
                            t.wait_ge(sems["ac"], g_ev[m - 2])
                        ins = None
                        for k in range(NKC):
                            ins = t.matmul(p_z[p][:, :],
                                           wbf[:, p * 2048 + k * 128:
                                               p * 2048 + (k + 1) * 128],
                                           xn_sl(k), start=(k == 0), stop=(k == NKC - 1))
                        g_mm.append(inc(ins, "pe"))
                    sect("tensor", f_t)

                    def f_s2(s, m=m, p=p):
                        s.wait_ge(sems["pe"], g_mm[m])
                        ins = s.activation(A1[:, (16 + m) * QT:(17 + m) * QT],
                                           p_z[p][:, :], AF.Sigmoid)
                        g_ev.append(inc(ins, "ac"))
                        note("scalar", ins)
                    sect("scalar", f_s2)
                state["stg_cast_last"] = ("ve", g_cast[-1])
                barrier()

        def ssd_phase(half, q):
            with (pt("pY0", [128, 128], F32) as pY0,
                  pt("pY1", [128, 128], F32) as pY1,
                  pt("pCB0", [128, 128], F32) as pCB0,
                  pt("pCB1", [128, 128], F32) as pCB1,
                  pt("p_sb", [128, 512], F32) as p_sb,
                  pt("pXt", [128, 128], BF16) as pXt,
                  pt("pBt", [128, 128], BF16) as pBt,
                  pt("pdH", [128, 128], F32) as pdH):
                pYs = [pY0, pY1]
                pCBs = [pCB0, pCB1]
                negla4 = dtw[:, 192:256]

                def f_t1(t, w=state["ssd_prev_yg"]):
                    t.wait_ge(sems["ve"], w)
                    inc(t.matmul(pY0[:, 0:64], tri01[:, :], negla4,
                                 start=True, stop=True), "pe")
                sect("tensor", f_t1)
                pe_c1 = C["pe"]

                def f_v1(v, wac=state["ssd_prev_ac"]):
                    v.wait_ge(sems["pe"], pe_c1)
                    v.wait_ge(sems["ac"], wac)
                    inc(v.tensor_copy(nsc_tok[:, :], pY0[:, 0:64]), "ve")
                sect("vector", f_v1)
                ve_d1 = C["ve"]

                def f_t2(t):
                    t.wait_ge(sems["ve"], ve_d1)
                    inc(t.transpose(pY1[0:64, :], nsc_tok[:, :], idf[:, :]), "pe")
                sect("tensor", f_t2)
                pe_c2 = C["pe"]

                def f_v2(v, wac=state["ssd_prev_ac"]):
                    v.wait_ge(sems["pe"], pe_c2)
                    v.wait_ge(sems["ac"], wac)
                    v.tensor_copy(nsc_hm[:, :], pY1[0:64, :])
                    v.drain()
                    inc(v.tensor_scalar(nb16[:, :], nsc_hm[:, 127:128], -1.0, None,
                                        op0=ALU.mult), "ve")
                sect("vector", f_v2)
                ve_d2 = C["ve"]

                def f_s1(s):
                    s.wait_ge(sems["ve"], ve_d2)
                    inc(s.activation(w_hm[:, :], nsc_hm[:, :], AF.Exp,
                                     bias=nb16[:, :], scale=1.0), "ac")
                sect("scalar", f_s1)
                ac_w = C["ac"]

                def f_t3(t):
                    t.wait_ge(sems["ac"], ac_w)
                    inc(t.transpose(pCB0[:, 0:64], w_hm[:, :], idf[0:64, 0:64]), "pe")
                sect("tensor", f_t3)
                pe_c4 = C["pe"]

                def f_v3(v):
                    v.wait_ge(sems["pe"], pe_c4)
                    inc(v.tensor_copy(w_tok[:, :], pCB0[:, 0:64]), "ve")
                sect("vector", f_v3)
                ve_d4q = C["ve"]
                state["ssd_prev_d4"] = ve_d4q

                e_ac_prev_chunk = state["ssd_prev_ac"]
                pv_xtok = ve_d4q
                pv_wb_free = ve_d4q
                pv_hup = ve_d4q
                ve_wb = ve_d4q
                prev_mp_last = ve_d4q
                prev_pe_sb = C["pe"]
                hist_mp = [ve_d4q, ve_d4q]
                hist_yg = [ve_d1, ve_d2]
                for kq in range(QT // Q):
                    pe_sb, sb_ve, sb_ac = [], [], []
                    mp_hist = []
                    for qq in range(4):
                        def f_g(g, kq=kq, qq=qq, prev_pe_sb=prev_pe_sb):
                            g.wait_ge(sems["ve"], ve_d2)
                            if qq >= 1:
                                g.wait_ge(sems["pe"], pe_sb[qq - 1])
                            else:
                                g.wait_ge(sems["pe"], prev_pe_sb)
                            inc(g.dma_start(
                                out=rowbuf[0:1, 0:512],
                                in_=nsc_hm[kq * 16 + qq * 4: kq * 16 + (qq + 1) * 4, :]),
                                "dG", 16)
                        sect("gpsimd", f_g)
                        dg_f = C["dG"]

                        def f_t4(t, qq=qq, dg_f=dg_f):
                            t.wait_ge(sems["dG"], dg_f)
                            if qq >= 1:
                                t.wait_ge(sems["ve"], sb_ve[qq - 1])
                                t.wait_ge(sems["ac"], sb_ac[qq - 1])
                            ins = t.matmul(p_sb[:, :], onesrow[:, :],
                                           rowbuf[0:1, 0:512],
                                           start=True, stop=True)
                            pe_sb.append(inc(ins, "pe"))
                        sect("tensor", f_t4)

                        def f_v4(v, kq=kq, qq=qq, wac=e_ac_prev_chunk):
                            v.wait_ge(sems["pe"], pe_sb[qq])
                            v.wait_ge(sems["ac"], wac)
                            v.scalar_tensor_tensor(
                                SBm[:, qq * 512:(qq + 1) * 512], p_sb[:, :], 1.0,
                                trimask[:, :].rearrange("p (a c) -> p a c", a=1)
                                .broadcast_to([128, 4, 128]),
                                op0=ALU.mult, op1=ALU.add)
                            v.tensor_scalar(alph[:, qq * 4:(qq + 1) * 4],
                                            p_sb[:, 127:512:128], 1.0, None,
                                            op0=ALU.mult)
                            v.drain()
                            ins = None
                            for hh in range(4):
                                h = qq * 4 + hh
                                ins = v.tensor_scalar(
                                    SBm[:, h * 128:(h + 1) * 128],
                                    SBm[:, h * 128:(h + 1) * 128],
                                    nsc_tok[:, kq * 16 + h: kq * 16 + h + 1], None,
                                    op0=ALU.subtract)
                            sb_ve.append(inc(ins, "ve"))
                        sect("vector", f_v4)

                        def f_s2(s, qq=qq, wve=state["ssd_prev_yg"], kq=kq,
                                 prev_mp_last=prev_mp_last):
                            s.wait_ge(sems["pe"], pe_sb[qq])
                            s.wait_ge(sems["ve"], wve)
                            s.wait_ge(sems["ve"], sb_ve[qq])
                            s.activation(expSB[:, qq * 512:(qq + 1) * 512],
                                         p_sb[:, :], AF.Exp,
                                         bias=cc[0:128, 3:4], scale=-1.0)
                            s.drain()
                            if qq >= 1:
                                s.wait_ge(sems["ve"], mp_hist[-1])
                            else:
                                s.wait_ge(sems["ve"], prev_mp_last)
                            s.activation(alph[:, qq * 4:(qq + 1) * 4],
                                         alph[:, qq * 4:(qq + 1) * 4], AF.Exp,
                                         bias=cc[0:128, 3:4], scale=-1.0)
                            s.drain()
                            ins = s.activation(t_e[:, :],
                                               SBm[:, qq * 512:(qq + 1) * 512],
                                               AF.Exp, bias=cc[0:128, 3:4],
                                               scale=-1.0)
                            sb_ac.append(inc(ins, "ac"))
                        sect("scalar", f_s2)
                        e_ready = C["ac"]

                        for hh in range(4):
                            h = qq * 4 + hh
                            base = (h % 2) * 64
                            jj = h // 2
                            x_sl = A1[:, 24 * HT + h * QT + kq * 128:
                                      24 * HT + h * QT + (kq + 1) * 128]
                            bcols = slice(jj * QT + kq * 128, jj * QT + (kq + 1) * 128)
                            ccols = slice(8 * QT + jj * QT + kq * 128,
                                          8 * QT + jj * QT + (kq + 1) * 128)
                            gcols = slice((16 + h) * QT + kq * 128,
                                          (16 + h) * QT + (kq + 1) * 128)

                            def f_t5(t, w=pv_xtok, x_sl=x_sl):
                                t.wait_ge(sems["ve"], w)
                                inc(t.transpose(pXt[:, :], x_sl, idbf[:, :]), "pe")
                            sect("tensor", f_t5)
                            pe_tx = C["pe"]

                            def f_v5(v, pe_tx=pe_tx, h=h):
                                v.wait_ge(sems["pe"], pe_tx)
                                slm = (h % 2) * 128
                                inc(v.tensor_copy(t_xtok[:, slm:slm + 128],
                                                  pXt[:, :]), "ve")
                            sect("vector", f_v5)
                            ve_xtok = C["ve"]

                            if h % 2 == 0:
                                def f_t6(t, w=pv_wb_free, bcols=bcols):
                                    t.wait_ge(sems["ve"], w)
                                    inc(t.transpose(pBt[:, :], A1[:, bcols],
                                                    idbf[:, :]), "pe")
                                sect("tensor", f_t6)
                                pe_tb = C["pe"]

                                def f_v6(v, pe_tb=pe_tb, h=h, kq=kq, jj=jj):
                                    v.wait_ge(sems["pe"], pe_tb)
                                    wc = kq * 16 + h
                                    sw = (jj % 2) * 128
                                    v.tensor_scalar(t_wb[:, sw:sw + 64], pBt[:, 0:64],
                                                    w_tok[:, wc:wc + 1], None,
                                                    op0=ALU.mult)
                                    inc(v.tensor_scalar(t_wb[:, sw + 64:sw + 128],
                                                        pBt[:, 64:128],
                                                        w_tok[:, wc + 1:wc + 2], None,
                                                        op0=ALU.mult), "ve")
                                sect("vector", f_v6)
                                ve_wb = C["ve"]

                            def f_t7(t, w=hist_mp[h % 2], bcols=bcols,
                                     ccols=ccols, base=base, h=h):
                                t.wait_ge(sems["ve"], w)
                                inc(t.matmul(pCBs[h % 2][:, :],
                                             A1[base:base + 64, bcols],
                                             A1[base:base + 64, ccols],
                                             start=True, stop=True), "pe")
                            sect("tensor", f_t7)
                            pe_cb = C["pe"]

                            def f_v7(v, pe_cb=pe_cb, ac_e=e_ready, hh=hh, h=h):
                                v.wait_ge(sems["pe"], pe_cb)
                                v.wait_ge(sems["ac"], ac_e)
                                inc(v.tensor_tensor(
                                    t_mp[:, (h % 2) * 128:(h % 2 + 1) * 128],
                                    pCBs[h % 2][:, :],
                                    t_e[:, hh * 128:(hh + 1) * 128],
                                    op=ALU.mult), "ve")
                            sect("vector", f_v7)
                            ve_mp = C["ve"]
                            mp_hist.append(ve_mp)
                            hist_mp[h % 2] = ve_mp

                            def f_v8(v, ccols=ccols, base=base, h=h, jj=jj,
                                     w=sb_ac[-1]):
                                sl2 = (jj % 2) * 128
                                v.wait_ge(sems["ac"], w)
                                v.tensor_tensor(t_cs[base:base + 64, sl2:sl2 + 128],
                                                A1[base:base + 64, ccols],
                                                expSB[base:base + 64,
                                                      h * 128: h * 128 + 128],
                                                op=ALU.mult)
                                inc(v.tensor_copy(t_hbf[base:base + 64, sl2:sl2 + 128],
                                                  Hp[base:base + 64,
                                                     jj * 128:(jj + 1) * 128]), "ve")
                            sect("vector", f_v8)
                            ve_cs = C["ve"]

                            def f_t8(t, wb=ve_wb, xt=ve_xtok, hu=pv_hup,
                                     base=base, h=h, jj=jj):
                                sw = (jj % 2) * 128
                                slm = (h % 2) * 128
                                t.wait_ge(sems["ve"], wb)
                                t.wait_ge(sems["ve"], xt)
                                t.wait_ge(sems["ve"], hu)
                                inc(t.matmul(pdH[base:base + 64, :],
                                             t_wb[:, sw + base:sw + base + 64],
                                             t_xtok[:, slm:slm + 128],
                                             start=True, stop=True,
                                             tile_position=(0, base)), "pe")
                            sect("tensor", f_t8)
                            pe_dh = C["pe"]

                            def f_t9(t, mp=ve_mp, cs=ve_cs, yg=hist_yg[h % 2],
                                     base=base, h=h, jj=jj):
                                sl2 = (jj % 2) * 128
                                slm = (h % 2) * 128
                                t.wait_ge(sems["ve"], mp)
                                t.wait_ge(sems["ve"], cs)
                                t.wait_ge(sems["ve"], yg)
                                t.matmul(pYs[h % 2][:, :],
                                         t_xtok[:, slm:slm + 128],
                                         t_mp[:, slm:slm + 128],
                                         start=True, stop=False)
                                inc(t.matmul(pYs[h % 2][:, :],
                                             t_hbf[base:base + 64, sl2:sl2 + 128],
                                             t_cs[base:base + 64, sl2:sl2 + 128],
                                             start=False, stop=True), "pe")
                            sect("tensor", f_t9)
                            pe_y = C["pe"]

                            def f_v9(v, pe_dh=pe_dh, pe_y=pe_y, base=base, h=h,
                                     jj=jj, gcols=gcols, wal=sb_ac[-1]):
                                v.wait_ge(sems["pe"], pe_dh)
                                v.wait_ge(sems["ac"], wal)
                                v.drain()
                                alpha = alph[base:base + 64, h:h + 1]
                                inc(v.scalar_tensor_tensor(
                                    Hp[base:base + 64, jj * 128:(jj + 1) * 128],
                                    Hp[base:base + 64, jj * 128:(jj + 1) * 128],
                                    alpha, pdH[base:base + 64, :],
                                    op0=ALU.mult, op1=ALU.add), "ve")
                                v.wait_ge(sems["pe"], pe_y)
                                v.drain()
                                ins = v.tensor_tensor(A1[:, gcols], pYs[h % 2][:, :],
                                                      A1[:, gcols], op=ALU.mult)
                                inc(ins, "ve")
                                note("vector", ins)
                            sect("vector", f_v9)
                            pv_hup = C["ve"] - 1
                            hist_yg[h % 2] = C["ve"]
                            pv_xtok = ve_xtok
                            pv_wb_free = ve_wb
                    e_ac_prev_chunk = sb_ac[-1]
                    prev_mp_last = mp_hist[-1]
                    prev_pe_sb = pe_sb[-1]
                    state["ssd_prev_yg"] = max(hist_yg)
                state["ssd_prev_ac"] = C["ac"]
                barrier()

        def rms_wout_phase(half, q):
            with (pt("p_s1", [1, 512], F32) as p_s1,
                  pt("p_rb", [128, 512], F32) as p_rb,
                  pt("p_w0", [128, 512], F32) as p_w0,
                  pt("p_w1", [128, 512], F32) as p_w1):
                p_w = [p_w0, p_w1]
                ts0 = q * QT
                pe_rms = []
                for h in range(NIC):
                    def f_v1(v, h=h):
                        if h > 0:
                            v.wait_ge(sems["pe"], pe_rms[h - 1])
                        gsl = A1[:, (16 + h) * QT:(17 + h) * QT]
                        inc(v.tensor_tensor(ysq_t[:, :], gsl, gsl, op=ALU.mult), "ve")
                    sect("vector", f_v1)
                    ve_y = C["ve"]

                    def f_t1(t, h=h, ve_y=ve_y):
                        t.wait_ge(sems["ve"], ve_y)
                        ins = t.matmul(p_s1[:, :], onescol[:, :], ysq_t[:, :],
                                       start=(h == 0), stop=(h == NIC - 1))
                        pe_rms.append(inc(ins, "pe"))
                    sect("tensor", f_t1)

                def f_s(s):
                    s.wait_ge(sems["pe"], pe_rms[-1])
                    inc(s.activation(rms_t[:, :], p_s1[:, :], AF.Sqrt,
                                     bias=cc[0:1, 2:3], scale=1.0 / DI), "ac")
                sect("scalar", f_s)
                ac_r = C["ac"]

                def f_v2(v):
                    v.wait_ge(sems["ac"], ac_r)
                    inc(v.reciprocal(rms_t[:, :], rms_t[:, :]), "ve")
                sect("vector", f_v2)
                ve_rr = C["ve"]

                def f_t2(t):
                    t.wait_ge(sems["ve"], ve_rr)
                    inc(t.matmul(p_rb[:, :], onesrow[:, :], rms_t[0:1, :],
                                 start=True, stop=True), "pe")
                sect("tensor", f_t2)
                pe_rb2 = C["pe"]

                yn_ve = []

                def f_v3(v):
                    v.wait_ge(sems["pe"], pe_rb2)
                    for h in range(NIC):
                        gsl = A1[:, (16 + h) * QT:(17 + h) * QT]
                        ins = v.scalar_tensor_tensor(gsl, gsl, normw[:, h:h + 1],
                                                     p_rb[:, :], op0=ALU.mult,
                                                     op1=ALU.mult)
                        yn_ve.append(inc(ins, "ve"))
                        v.drain()
                sect("vector", f_v3)

                wo_dma, wo_cast, wo_mm, wo_ev = [], [], [], []
                for o in range(NKC):
                    p = o % 2

                    def f_q(g, o=o, p=p):
                        if o >= 2:
                            g.wait_ge(sems["ve"], wo_cast[o - 2])
                        else:
                            g.wait_ge(sems[state["stg_cast_last"][0]], state["stg_cast_last"][1])
                        ins = g.dma_start(out=stg[:, p * 2048:(p + 1) * 2048],
                                          in_=wout[o, :, :])
                        wo_dma.append(inc(ins, f"dW{p}", 16))
                    sect("sync", f_q)

                    def f_c(v, o=o, p=p):
                        v.wait_ge(sems[f"dW{p}"], wo_dma[o])
                        if o >= 2:
                            v.wait_ge(sems["pe"], wo_mm[o - 2])
                        ins = v.tensor_copy(wbf[:, p * 2048:(p + 1) * 2048],
                                            stg[:, p * 2048:(p + 1) * 2048])
                        wo_cast.append(inc(ins, "ve"))
                    sect("vector", f_c)

                    def f_t(t, o=o, p=p):
                        t.wait_ge(sems["ve"], wo_cast[o])
                        t.wait_ge(sems["ve"], yn_ve[-1])
                        if o >= 2:
                            t.wait_ge(sems["ve"], wo_ev[o - 2])
                        ins = None
                        for kk in range(NIC):
                            ins = t.matmul(p_w[p][:, :],
                                           wbf[:, p * 2048 + kk * 128:
                                               p * 2048 + (kk + 1) * 128],
                                           A1[:, (16 + kk) * QT:(17 + kk) * QT],
                                           start=(kk == 0), stop=(kk == NIC - 1))
                        wo_mm.append(inc(ins, "pe"))
                    sect("tensor", f_t)

                    def f_v4(v, o=o, p=p):
                        v.wait_ge(sems["pe"], wo_mm[o])
                        asl = acc[:, o * HT + ts0: o * HT + ts0 + QT]
                        ins = v.tensor_tensor(asl, p_w[p][:, :], asl, op=ALU.add)
                        wo_ev.append(inc(ins, "ve"))
                        v.drain()
                        note("vector", ins)
                    sect("vector", f_v4)
                state["stg_cast_last"] = ("ve", wo_cast[-1])
                barrier()

        def router_phase(half):
            with (pt("p_lg", [8, 1024], F32) as p_lg,
                  pt("p_tok0", [128, 8], F32) as p_tok0,
                  pt("p_tok1", [128, 8], F32) as p_tok1,
                  pt("p_st0", [8, 128], F32) as p_st0,
                  pt("p_st1", [8, 128], F32) as p_st1,
                  pt("p_sb", [128, 512], F32) as p_sb,
                  pt("p_sb2", [128, 512], F32) as p_sb2):
                p_tok = [p_tok0, p_tok1]
                p_st = [p_st0, p_st1]

                def f_t(t):
                    ins = None
                    for tsl in range(2):
                        for k in range(NKC):
                            ins = t.matmul(p_lg[:, tsl * 512:(tsl + 1) * 512],
                                           wrout_bf[:, k * 8:(k + 1) * 8],
                                           xn[:, k * HT + tsl * 512:
                                              k * HT + (tsl + 1) * 512],
                                           start=(k == 0), stop=(k == NKC - 1))
                    inc(ins, "pe")
                sect("tensor", f_t)
                pe_lgv = C["pe"]

                def f_v(v):
                    v.wait_ge(sems["pe"], pe_lgv)
                    inc(v.tensor_scalar(lt_r[:, :], p_lg[:, :], routb[:, :], None,
                                        op0=ALU.add), "ve")
                sect("vector", f_v)
                ve_lt = C["ve"]

                exp_ac, st_ve = [], []
                for tc in range(HT // 128):
                    p = tc % 2

                    def f_t1(t, tc=tc, p=p):
                        t.wait_ge(sems["ve"], ve_lt)
                        if tc >= 2:
                            t.wait_ge(sems["ac"], exp_ac[tc - 2])
                        inc(t.transpose(p_tok[p][:, :],
                                        lt_r[:, tc * 128:(tc + 1) * 128],
                                        idf[0:8, 0:8]), "pe")
                    sect("tensor", f_t1)
                    pe_tt = C["pe"]

                    def f_v1(v, pe_tt=pe_tt, p=p):
                        v.wait_ge(sems["pe"], pe_tt)
                        inc(v.tensor_reduce(r_sc[:, 0:1], p_tok[p][:, :], axis=AXX,
                                            op=ALU.max, negate=True), "ve")
                    sect("vector", f_v1)
                    ve_nm = C["ve"]

                    def f_s1(s, ve_nm=ve_nm, p=p):
                        s.wait_ge(sems["ve"], ve_nm)
                        ins = s.activation(e_t[:, :], p_tok[p][:, :], AF.Exp,
                                           bias=r_sc[:, 0:1], scale=1.0)
                        exp_ac.append(inc(ins, "ac"))
                    sect("scalar", f_s1)
                    ac_ex = C["ac"]

                    def f_v2(v, ac_ex=ac_ex):
                        v.wait_ge(sems["ac"], ac_ex)
                        v.tensor_reduce(r_sc[:, 1:2], e_t[:, :], axis=AXX, op=ALU.add)
                        v.drain()
                        v.reciprocal(r_sc[:, 2:3], r_sc[:, 1:2])
                        v.tensor_scalar(mask_t[:, :], e_t[:, :], 1.0, None,
                                        op0=ALU.is_equal)
                        v.drain()
                        inc(v.tensor_scalar(mask_t[:, :], mask_t[:, :], r_sc[:, 2:3],
                                            None, op0=ALU.mult), "ve")
                    sect("vector", f_v2)
                    ve_msk = C["ve"]

                    def f_t2(t, ve_msk=ve_msk, tc=tc, p=p):
                        t.wait_ge(sems["ve"], ve_msk)
                        if tc >= 2:
                            t.wait_ge(sems["ve"], st_ve[tc - 2])
                        inc(t.transpose(p_st[p][:, :], mask_t[:, :], idf[:, :]), "pe")
                    sect("tensor", f_t2)
                    pe_st2 = C["pe"]

                    def f_v3(v, pe_st2=pe_st2, tc=tc, p=p):
                        v.wait_ge(sems["pe"], pe_st2)
                        ins = v.tensor_copy(s_T[0:8, tc * 128:(tc + 1) * 128],
                                            p_st[p][:, :])
                        st_ve.append(inc(ins, "ve"))
                    sect("vector", f_v3)

                state["s_T_ready"] = st_ve[-1]
                barrier()

        def ffn_phase(half):
            for e in range(NE):
                def f_vb(v, e=e):
                    v.wait_ge(sems["ve"], state["s_T_ready"])
                    if state["shtmp_guard"]:
                        v.wait_ge(sems["dG"], state["shtmp_guard"])
                    ins = v.stream_shuffle(shtmp[:, :], s_T[:, :], [e] * 32)
                    inc(ins, "ve")
                sect("vector", f_vb)
                sh_rdy = C["ve"]

                def f_gb(g, e=e, sh_rdy=sh_rdy):
                    g.wait_ge(sems["ve"], sh_rdy)
                    if state["sb2_guard"][e % 2]:
                        g.wait_ge(sems["ve"], state["sb2_guard"][e % 2])
                    ins = g.dma_start(
                        out=s_b2[0:128:32, (e % 2) * HT:(e % 2 + 1) * HT],
                        in_=shtmp[0:4, :])
                    state["shtmp_guard"] = inc(ins, "dG", 16)
                sect("gpsimd", f_gb)
                dg_b = C["dG"]

                def f_vb2(v, e=e, dg_b=dg_b):
                    v.wait_ge(sems["dG"], dg_b)
                    sl = s_b2[:, (e % 2) * HT:(e % 2 + 1) * HT]
                    ins = v.stream_shuffle(sl, sl, [0] * 32)
                    inc(ins, "ve")
                sect("vector", f_vb2)
                sb2_rdy = C["ve"]
                with (pt("p_fa0", [128, 1024], F32) as p_fa0,
                      pt("p_fg0", [128, 1024], F32) as p_fg0,
                      pt("p_fa1", [128, 1024], F32) as p_fa1,
                      pt("p_fg1", [128, 1024], F32) as p_fg1):
                    p_fa = [p_fa0, p_fa1]
                    p_fg = [p_fg0, p_fg1]
                    pref = state.get("ffn_pref")
                    if pref is not None:
                        w1_dma = list(pref[0])
                        w1_cast = list(pref[1])
                        state["ffn_pref"] = None
                    else:
                        w1_dma, w1_cast = [], []
                    w1_amm, w1_gmm, w1_hg, w1_silu = [], [], [], []

                    def emit_hg(jh):
                        ph = jh % 2

                        def f_v(v, jh=jh, ph=ph):
                            v.wait_ge(sems["ac"], w1_silu[jh])
                            v.wait_ge(sems["pe"], w1_gmm[jh])
                            ins = v.tensor_tensor(
                                A1[:, jh * HT:(jh + 1) * HT],
                                wbf[:, ph * 2048: ph * 2048 + 1024],
                                p_fg[ph][:, :], op=ALU.mult)
                            w1_hg.append(inc(ins, "ve"))
                        sect("vector", f_v)

                    for j in range(32):
                        p = j % 2
                        if j < len(w1_dma):
                            pass
                        else:
                            def f_q(g, j=j, p=p, e=e):
                                if j >= 2:
                                    g.wait_ge(sems["ve"], w1_cast[j - 2])
                                else:
                                    g.wait_ge(sems[state["stg_cast_last"][0]],
                                              state["stg_cast_last"][1])
                                ins = g.dma_start(
                                    out=stg[:, p * 2048:(p + 1) * 2048],
                                    in_=w1t[e, :, j * 2048:(j + 1) * 2048])
                                w1_dma.append(inc(ins, f"dW{p}", 16))
                            sect("sync", f_q)

                            def f_cv(v, j=j, p=p, phg=state["prev_ffn_hg"]):
                                v.wait_ge(sems[f"dW{p}"], w1_dma[j])
                                if j >= 2:
                                    v.wait_ge(sems["ve"], w1_hg[j - 2])
                                elif phg:
                                    v.wait_ge(sems["ve"], phg)
                                ins = v.tensor_copy(wbf[:, p * 2048:(p + 1) * 2048],
                                                    stg[:, p * 2048:(p + 1) * 2048])
                                w1_cast.append(inc(ins, "ve"))
                            sect("vector", f_cv)

                        def f_t(t, j=j, p=p, pev=tuple(state["prev_ffn_ev_pair"])):
                            t.wait_ge(sems["ve"], w1_cast[j])
                            if j >= 2:
                                t.wait_ge(sems["ac"], w1_silu[j - 2])
                                t.wait_ge(sems["ve"], w1_hg[j - 2])
                            elif pev[p]:
                                t.wait_ge(sems["ve"], pev[p])
                            ins = None
                            for k in range(NKC):
                                for tsl in range(2):
                                    ins = t.matmul(
                                        p_fa[p][:, tsl * 512:(tsl + 1) * 512],
                                        wbf[:, p * 2048 + k * 128:
                                            p * 2048 + (k + 1) * 128],
                                        xn[:, k * HT + tsl * 512:
                                           k * HT + (tsl + 1) * 512],
                                        start=(k == 0), stop=(k == NKC - 1))
                            w1_amm.append(inc(ins, "pe"))
                            for k in range(NKC):
                                for tsl in range(2):
                                    ins = t.matmul(
                                        p_fg[p][:, tsl * 512:(tsl + 1) * 512],
                                        wbf[:, p * 2048 + 1024 + k * 128:
                                            p * 2048 + 1024 + (k + 1) * 128],
                                        xn[:, k * HT + tsl * 512:
                                           k * HT + (tsl + 1) * 512],
                                        start=(k == 0), stop=(k == NKC - 1))
                            w1_gmm.append(inc(ins, "pe"))
                        sect("tensor", f_t)

                        def f_s2(s, j=j, p=p):
                            s.wait_ge(sems["pe"], w1_amm[j])
                            ins = s.activation(wbf[:, p * 2048: p * 2048 + 1024],
                                               p_fa[p][:, :], AF.Silu)
                            w1_silu.append(inc(ins, "ac"))
                        sect("scalar", f_s2)

                        if j >= 1:
                            emit_hg(j - 1)
                    emit_hg(31)

                with contextlib.ExitStack() as fs:
                    p_fo = [fs.enter_context(
                        pt(f"p_fo{i}", [128, 512], F32)) for i in range(8)]
                    ev_done = []
                    prev_cast = None
                    pend_w2 = state.get("w2_pend")
                    state["w2_pend"] = None
                    for og in range(4):
                        gp_par = og % 2
                        if og == 0 and pend_w2 is not None:
                            w2_dma = list(pend_w2[0])
                            w2_cast = list(pend_w2[1])
                        elif og > 0 and state.get("w2_pend_next"):
                            w2_dma = list(state["w2_pend_next"][0])
                            w2_cast = list(state["w2_pend_next"][1])
                            state["w2_pend_next"] = None
                        else:
                            w2_dma, w2_cast = [], []
                        w2_mm = []

                        def emit_w2_dc(j, tgt_e, tgt_og, dma_l, cast_l,
                                       dma_guard, cast_guard):
                            s4 = j % 4

                            def f_q(g, j=j, s4=s4, dg=dma_guard):
                                if dg:
                                    g.wait_ge(sems["ve"], dg)
                                ins = g.dma_start(
                                    out=stgw2[:, s4 * 256:(s4 + 1) * 256],
                                    in_=w2t[tgt_e, tgt_og, :,
                                            j * 256:(j + 1) * 256])
                                dma_l.append(inc(ins, f"dW{2 + s4}", 16))
                            sect("sync", f_q)

                            def f_c(v, j=j, s4=s4, cg=cast_guard):
                                v.wait_ge(sems[f"dW{2 + s4}"], dma_l[j])
                                if cg:
                                    v.wait_ge(sems["pe"], cg)
                                ins = v.tensor_copy(
                                    w2bf[:, s4 * 256:(s4 + 1) * 256],
                                    stgw2[:, s4 * 256:(s4 + 1) * 256])
                                cast_l.append(inc(ins, "ve"))
                            sect("vector", f_c)

                        for j in range(32):
                            s4 = j % 4
                            if j < len(w2_dma):
                                pass
                            elif j < 4:
                                emit_w2_dc(j, e, og, w2_dma, w2_cast,
                                           state["prev_w2_cast"],
                                           state["prev_ffn_mm"] if og == 0 else None)
                            else:
                                emit_w2_dc(j, e, og, w2_dma, w2_cast,
                                           w2_cast[j - 4], w2_mm[j - 4])

                            def f_t(t, j=j, s4=s4, og=og, gp_par=gp_par):
                                t.wait_ge(sems["ve"], w2_cast[j])
                                if og == 0:
                                    t.wait_ge(sems["ve"], w1_hg[j])
                                if j == 0:
                                    if og >= 2:
                                        t.wait_ge(sems["ve"], ev_done[og - 2])
                                    else:
                                        t.wait_ge(sems["ac"], w1_silu[-1] if gp_par
                                                  else w1_silu[-2])
                                        t.wait_ge(sems["ve"], w1_hg[-1] if gp_par
                                                  else w1_hg[-2])
                                ins = None
                                for o2 in range(2):
                                    for tsl in range(2):
                                        ins = t.matmul(
                                            p_fo[gp_par * 4 + o2 * 2 + tsl][:, :],
                                            w2bf[:, s4 * 256 + o2 * 128:
                                                 s4 * 256 + (o2 + 1) * 128],
                                            A1[:, j * HT + tsl * 512:
                                               j * HT + (tsl + 1) * 512],
                                            start=(j == 0), stop=(j == 31))
                                w2_mm.append(inc(ins, "pe"))
                            sect("tensor", f_t)
                        prev_cast = w2_cast
                        nxt = None
                        if og < 3:
                            nxt = (e, og + 1)
                        elif e < NE - 1:
                            nxt = (e + 1, 0)
                        if nxt is not None:
                            n_dma, n_cast = [], []
                            for jn in range(4):
                                emit_w2_dc(jn, nxt[0], nxt[1], n_dma, n_cast,
                                           w2_cast[28 + jn], w2_mm[28 + jn])
                            if og < 3:
                                state["w2_pend_next"] = (n_dma, n_cast)
                            else:
                                state["w2_pend"] = (n_dma, n_cast)
                        if og == 2 and e < NE - 1:
                            pf_dma, pf_cast = [], []
                            for jp in range(2):
                                def f_qp(g, jp=jp, e=e):
                                    g.wait_ge(sems["ve"], w1_cast[30 + jp])
                                    ins = g.dma_start(
                                        out=stg[:, jp * 2048:(jp + 1) * 2048],
                                        in_=w1t[e + 1, :, jp * 2048:(jp + 1) * 2048])
                                    pf_dma.append(inc(ins, f"dW{jp}", 16))
                                sect("sync", f_qp)

                                def f_cp(v, jp=jp):
                                    v.wait_ge(sems[f"dW{jp}"], pf_dma[jp])
                                    v.wait_ge(sems["ve"], w1_hg[-1])
                                    ins = v.tensor_copy(
                                        wbf[:, jp * 2048:(jp + 1) * 2048],
                                        stg[:, jp * 2048:(jp + 1) * 2048])
                                    pf_cast.append(inc(ins, "ve"))
                                sect("vector", f_cp)
                            state["ffn_pref"] = (pf_dma, pf_cast)

                        def f_v2(v, e=e, og=og, gp_par=gp_par, wlast=w2_mm[-1],
                                 sb2_rdy=sb2_rdy):
                            v.wait_ge(sems["pe"], wlast)
                            v.wait_ge(sems["ve"], sb2_rdy)
                            ins = None
                            for o2 in range(2):
                                o = og * 2 + o2
                                for tsl in range(2):
                                    asl = acc[:, o * HT + tsl * 512:
                                              o * HT + (tsl + 1) * 512]
                                    ssl = s_b2[:, (e % 2) * HT + tsl * 512:
                                               (e % 2) * HT + (tsl + 1) * 512]
                                    v.tensor_tensor(tmp1024[:, 0:512],
                                                    p_fo[gp_par * 4 + o2 * 2 + tsl][:, :],
                                                    ssl, op=ALU.mult)
                                    v.drain()
                                    ins = v.tensor_tensor(asl, tmp1024[:, 0:512],
                                                          asl, op=ALU.add)
                                    v.drain()
                            ev_done.append(inc(ins, "ve"))
                            note("vector", ins)
                        sect("vector", f_v2)
                        state["prev_ffn_mm"] = w2_mm[-1]
                    state["prev_ffn_ev"] = ev_done[-1]
                    state["prev_ffn_ev_pair"] = [ev_done[-2], ev_done[-1]]
                    state["prev_w2_cast"] = prev_cast[-1]
                    state["sb2_guard"][e % 2] = ev_done[-1]
                    state["stg_cast_last"] = ("ve", w1_cast[-1])
                    state["prev_ffn_hg"] = w1_hg[-1]
            barrier()

        def store_phase(half):
            tgt = {s: C[s] for s in ("pe", "ve", "ac", "gp")}

            def f_g(g):
                for s in ("pe", "ve", "ac", "gp"):
                    if tgt[s] > 0:
                        g.wait_ge(sems[s], tgt[s])
                for cch in range(NKC):
                    inc(g.dma_start(
                        out=out_t[cch * 128:(cch + 1) * 128,
                                  half * HT:(half + 1) * HT],
                        in_=acc[:, cch * HT:(cch + 1) * HT]), "dS", 16)
            sect("gpsimd", f_g)

        init_phase()
        for half in range(n_half):
            for q in range(2):
                ln_phase(half, q)
                z_phase(half, q)
                gate_phase(half, q)
                ssd_phase(half, q)
                rms_wout_phase(half, q)
            router_phase(half)
            ffn_phase(half)
            store_phase(half)
        ds_t = C["dS"]
        sect("gpsimd", lambda g: g.wait_ge(sems["dS"], ds_t))

    return nc


def host_prep(inputs, n_half=2):
    LT = n_half * HT
    W_in = np.asarray(inputs["L0_attn_W_in"], dtype=np.float32)
    conv_w = np.asarray(inputs["L0_attn_conv_w"], dtype=np.float32)
    conv_b = np.asarray(inputs["L0_attn_conv_b"], dtype=np.float32)
    dp = np.asarray(inputs["L0_attn_delta_param"], dtype=np.float32)
    W_gate = np.asarray(inputs["L0_attn_W_gate"], dtype=np.float32)
    norm_w = np.asarray(inputs["L0_attn_norm_w"], dtype=np.float32)
    W_out = np.asarray(inputs["L0_attn_W_out"], dtype=np.float32)
    r_w = np.asarray(inputs["L0_ffn_router_w"], dtype=np.float32)
    r_b = np.asarray(inputs["L0_ffn_router_b"], dtype=np.float32)
    w1 = np.asarray(inputs["L0_ffn_w1"], dtype=np.float32)
    w2 = np.asarray(inputs["L0_ffn_w2"], dtype=np.float32)

    ctg = np.ascontiguousarray
    shared = {}
    shared["win_x"] = ctg(W_in[:, :2048].reshape(8, 128, 16, 128)
                          .transpose(2, 1, 0, 3).reshape(16, 128, 1024))
    shared["win_b"] = ctg(W_in[:, 2048:3072].reshape(8, 128, 16, 64)
                          .transpose(2, 1, 0, 3).reshape(16, 128, 512))
    shared["win_c"] = ctg(W_in[:, 3072:4096].reshape(8, 128, 16, 64)
                          .transpose(2, 1, 0, 3).reshape(16, 128, 512))
    shared["win_dt"] = ctg(W_in[:, 4096:4112].reshape(8, 128, 16)
                           .transpose(1, 0, 2).reshape(128, 128))
    shared["wgate"] = ctg(W_gate.reshape(8, 128, 16, 128)
                          .transpose(2, 1, 0, 3).reshape(16, 128, 1024))
    shared["wout"] = ctg(W_out.reshape(16, 128, 8, 128)
                         .transpose(2, 1, 0, 3).reshape(8, 128, 2048))
    shared["wrout"] = ctg(r_w.reshape(8, 128, 8).transpose(1, 0, 2).reshape(128, 64))
    order = np.empty(64, dtype=np.int64)
    order[0::2] = np.arange(32)
    order[1::2] = np.arange(32, 64)
    w1r = w1.reshape(8, 8, 128, 64, 128).transpose(0, 2, 3, 1, 4)
    shared["w1t"] = ctg(w1r[:, :, order].reshape(8, 128, 65536))
    w2r = w2.reshape(8, 32, 128, 4, 2, 128).transpose(0, 3, 2, 1, 4, 5)
    shared["w2t"] = ctg(w2r.reshape(8, 4, 128, 8192))
    ident = np.eye(128, dtype=np.float32)
    shared["c_idbf"] = ident.astype(BF)
    shared["c_idf"] = ident
    sidx = np.arange(128)
    shared["c_tri01"] = (sidx[:, None] <= sidx[None, :]).astype(np.float32)
    shared["c_trimask"] = (NEG_BIG * (sidx[:, None] > sidx[None, :])).astype(np.float32)
    shared["c_onesrow"] = np.ones((1, 128), dtype=np.float32)
    shared["c_onescol"] = np.ones((128, 1), dtype=np.float32).astype(BF)
    cc = np.zeros((128, 4), dtype=np.float32)
    cc[:, 0] = 1.0
    cc[:, 1] = LN_EPS
    cc[:, 2] = RMS_EPS
    shared["c_cc"] = cc
    shared["c_normw"] = ctg(norm_w.reshape(16, 128).T)
    shared["c_convb"] = ctg(conv_b.reshape(16, 128).T)
    shared["c_convw"] = ctg(conv_w.reshape(16, 128, 4).transpose(1, 0, 2)
                            .reshape(128, 64))
    shared["c_dptile"] = np.tile(dp, (128, 4)).astype(np.float32)
    shared["c_routb"] = r_b.reshape(8, 1)
    return shared


def kernel(**inputs):
    n_half = 2
    u = np.asarray(inputs["L0_attn_u"], dtype=np.float32)
    ncores = u.shape[0]
    shared = host_prep(inputs, n_half)
    nc = build_program(n_half)
    in_maps = []
    for c in range(ncores):
        m = dict(shared)
        m["u_t"] = np.ascontiguousarray(u[c].T)
        in_maps.append(m)
    res = run_bass_kernel_spmd(nc, in_maps, core_ids=list(range(ncores)))
    out = np.stack([np.ascontiguousarray(r["out_t"].T) for r in res.results], axis=0)
    return out


# version 11
# speedup vs baseline: 1.0881x; 1.0811x over previous
import numpy as np
import ml_dtypes
import contextlib
import concourse.bass as bass
import concourse.mybir as mybir
from concourse.bass_utils import run_bass_kernel_spmd

F32 = mybir.dt.float32
BF16 = mybir.dt.bfloat16
AF = mybir.ActivationFunctionType
ALU = mybir.AluOpType
BF = ml_dtypes.bfloat16
AXX = mybir.AxisListType.X

D = 1024
DI = 2048
NH = 16
DC = 4
Q = 128
QT = 512
HT = 1024
NKC = 8
NIC = 16
NE = 8
LN_EPS = 1e-5
RMS_EPS = 1.1920928955078125e-07
NEG_BIG = 1e30


def build_program(n_half=2):
    nc = bass.Bass("TRN2", target_bir_lowering=False, debug=False)
    LT = n_half * HT

    u_t = nc.dram_tensor("u_t", [D, LT], F32, kind="ExternalInput").ap()
    out_t = nc.dram_tensor("out_t", [D, LT], F32, kind="ExternalOutput").ap()
    win_x = nc.dram_tensor("win_x", [NIC, 128, 1536], F32, kind="ExternalInput").ap()
    win_b = nc.dram_tensor("win_b", [8, 128, 1024], F32, kind="ExternalInput").ap()
    win_c = nc.dram_tensor("win_c", [8, 128, 1024], F32, kind="ExternalInput").ap()
    win_dt = nc.dram_tensor("win_dt", [128, 128], F32, kind="ExternalInput").ap()
    wgate = nc.dram_tensor("wgate", [NIC, 128, 1024], F32, kind="ExternalInput").ap()
    wout = nc.dram_tensor("wout", [NKC, 128, 2048], F32, kind="ExternalInput").ap()
    wrout = nc.dram_tensor("wrout", [128, 64], F32, kind="ExternalInput").ap()
    w1t = nc.dram_tensor("w1t", [NE, 128, 65536], F32, kind="ExternalInput").ap()
    w2t = nc.dram_tensor("w2t", [NE, 4, 128, 8192], F32, kind="ExternalInput").ap()
    c_idbf = nc.dram_tensor("c_idbf", [128, 128], BF16, kind="ExternalInput").ap()
    c_idf = nc.dram_tensor("c_idf", [128, 128], F32, kind="ExternalInput").ap()
    c_tri01 = nc.dram_tensor("c_tri01", [128, 128], F32, kind="ExternalInput").ap()
    c_trimask = nc.dram_tensor("c_trimask", [128, 128], F32, kind="ExternalInput").ap()
    c_onesrow = nc.dram_tensor("c_onesrow", [1, 128], F32, kind="ExternalInput").ap()
    c_onescol = nc.dram_tensor("c_onescol", [128, 1], BF16, kind="ExternalInput").ap()
    c_cc = nc.dram_tensor("c_cc", [128, 4], F32, kind="ExternalInput").ap()
    c_normw = nc.dram_tensor("c_normw", [128, 16], F32, kind="ExternalInput").ap()
    c_convb = nc.dram_tensor("c_convb", [128, 16], F32, kind="ExternalInput").ap()
    c_convw = nc.dram_tensor("c_convw", [128, 64], F32, kind="ExternalInput").ap()
    c_dptile = nc.dram_tensor("c_dptile", [128, 64], F32, kind="ExternalInput").ap()
    c_routb = nc.dram_tensor("c_routb", [8, 1], F32, kind="ExternalInput").ap()

    es = contextlib.ExitStack()
    with es:
        sb = lambda name, shape, dt: es.enter_context(nc.sbuf_tensor(name, shape, dt))
        acc = sb("acc", [128, NKC * HT], F32)
        xn = sb("xn", [128, NKC * HT], BF16)
        A1 = sb("A1", [128, 32 * HT], BF16)
        stg = sb("stg", [128, 2 * 2048], F32)
        wbf = sb("wbf", [128, 2 * 2048], BF16)
        stgw2 = sb("stgw2", [128, 1024], F32)
        w2bf = sb("w2bf", [128, 1024], BF16)
        x_pre = sb("x_pre", [128, QT + 3], BF16)
        tails = sb("tails", [128, NIC * 3], BF16)
        SBm = sb("SBm", [128, NH * Q], F32)
        expSB = sb("expSB", [128, NH * Q], BF16)
        alph = sb("alph", [128, 16], F32)
        dtw = sb("dtw", [128, 4 * 64], F32)
        nsc_tok = sb("nsc_tok", [128, 64], F32)
        nsc_hm = sb("nsc_hm", [64, 128], F32)
        w_hm = sb("w_hm", [64, 128], F32)
        w_tok = sb("w_tok", [128, 64], F32)
        nb16 = sb("nb16", [64, 1], F32)
        Hp = sb("Hp", [128, 8 * 128], F32)
        t_xtok = sb("t_xtok", [128, 256], BF16)
        t_wb = sb("t_wb", [128, 256], BF16)
        t_mp = sb("t_mp", [128, 256], BF16)
        t_cs = sb("t_cs", [128, 256], BF16)
        t_e = sb("t_e", [128, 512], F32)
        t_hbf = sb("t_hbf", [128, 256], BF16)
        rowbuf = sb("rowbuf", [1, 512], F32)
        s_T = sb("s_T", [32, HT], BF16)
        s_b2 = sb("s_b2", [128, 2 * HT], BF16)
        shtmp = sb("shtmp", [32, HT], BF16)
        ln_t = sb("ln_t", [1, 3 * QT], F32)
        rms_t = sb("rms_t", [1, QT], F32)
        tmp1024 = sb("tmp1024", [128, 1024], F32)
        ubf = sb("ubf", [128, 2 * 512], BF16)
        usq = sb("usq", [128, 2 * 512], BF16)
        ysq_t = sb("ysq_t", [128, 512], BF16)
        lt_r = sb("lt_r", [8, HT], F32)
        e_t = sb("e_t", [128, 8], F32)
        mask_t = sb("mask_t", [128, 8], F32)
        r_sc = sb("r_sc", [128, 4], F32)
        idbf = sb("idbf", [128, 128], BF16)
        idf = sb("idf", [128, 128], F32)
        tri01 = sb("tri01", [128, 128], F32)
        trimask = sb("trimask", [128, 128], F32)
        onesrow = sb("onesrow", [1, 128], F32)
        onescol = sb("onescol", [128, 1], BF16)
        cc = sb("cc", [128, 4], F32)
        normw = sb("normw", [128, 16], F32)
        convb = sb("convb", [128, 16], F32)
        convw = sb("convw", [128, 64], F32)
        dptile = sb("dptile", [128, 64], F32)
        routb = sb("routb", [8, 1], F32)
        wrout_s = sb("wrout_s", [128, 64], F32)
        wrout_bf = sb("wrout_bf", [128, 64], BF16)
        wdt_bf = sb("wdt_bf", [128, 128], BF16)

        sems = {}
        for name in ("dW0", "dW1", "dW2", "dW3", "dW4", "dW5", "dG", "dS", "pe", "ve", "ac", "gp"):
            sems[name] = es.enter_context(nc.semaphore(name))
        block = es.enter_context(nc.Block())

        C = {k: 0 for k in sems}
        _updated = set()
        _updated_refs = []
        OWN = {"tensor": "pe", "scalar": "ac", "vector": "ve", "gpsimd": "gp"}

        def inc(ins, s, amt=1):
            assert id(ins) not in _updated, "instruction already carries an update"
            _updated.add(id(ins))
            _updated_refs.append(ins)
            ins.then_inc(sems[s], amt)
            C[s] += amt
            return C[s]

        def sect(engname, fn):
            deco = getattr(block, engname)

            @deco
            def _(eng):
                fn(eng)

        def note(engname, ins):
            if id(ins) not in _updated:
                inc(ins, OWN[engname])
            return ins

        def barrier():
            tgt = {s: C[s] for s in ("pe", "ve", "ac", "gp")}

            def f(e):
                for s in ("pe", "ve", "ac", "gp"):
                    if tgt[s] > 0:
                        e.wait_ge(sems[s], tgt[s])
            for engname in ("tensor", "scalar", "vector", "gpsimd"):
                sect(engname, f)

        state = {"prev_ffn_ev": 0, "prev_ffn_mm": 0, "stg_cast_last": ("ac", 0),
                 "ssd_prev_ac": 0, "ssd_prev_yg": 0, "ssd_prev_d4": 0,
                 "s_T_ready": 0, "sb2_ready": 0, "sb2_guard": [0, 0],
                 "prev_ffn_ev_pair": [0, 0], "prev_w2_cast": 0, "prev_ffn_hg": 0,
                 "shtmp_guard": 0}
        _uid = [0]

        def pt(name, shape, dt):
            _uid[0] += 1
            return nc.psum_tensor(f"{name}_{_uid[0]}", shape, dt)

        def init_phase():
            def f_g(g):
                for dst, src in [(idbf, c_idbf), (idf, c_idf), (tri01, c_tri01),
                                 (trimask, c_trimask), (onesrow, c_onesrow),
                                 (onescol, c_onescol), (cc, c_cc), (normw, c_normw),
                                 (convb, c_convb), (convw, c_convw), (dptile, c_dptile),
                                 (routb, c_routb), (wrout_s, wrout)]:
                    inc(g.dma_start(out=dst[:, :], in_=src[:, :]), "dG", 16)
                inc(g.dma_start(out=stg[:, 0:128], in_=win_dt[:, :]), "dG", 16)
            sect("gpsimd", f_g)
            dg0 = C["dG"]

            def f_s(s):
                s.wait_ge(sems["dG"], dg0)
                s.copy(wdt_bf[:, :], stg[:, 0:128])
                ins = s.copy(wrout_bf[:, :], wrout_s[:, :])
                state["stg_cast_last"] = ("ac", inc(ins, "ac"))
                note("scalar", ins)
            sect("scalar", f_s)

            def f_v(v):
                v.wait_ge(sems["dG"], dg0)
                v.memset(Hp[:, :], 0.0)
                v.memset(s_T[:, :], 0.0)
                v.memset(s_b2[:, :], 0.0)
                note("vector", v.memset(tails[:, :], 0.0))
            sect("vector", f_v)
            barrier()

        def ln_phase(half, q):
            with (pt("p_s1", [1, 512], F32) as p_s1,
                  pt("p_s2", [1, 512], F32) as p_s2,
                  pt("p_rb", [128, 512], F32) as p_rb,
                  pt("p_mrb", [128, 512], F32) as p_mrb):
                ts0 = q * QT
                if q == 0:
                    st_w = C["dS"]
                    dg_before = C["dG"]

                    def f_g(g):
                        g.wait_ge(sems["dS"], st_w)
                        g.wait_ge(sems["dG"], dg_before)
                        for cch in range(NKC):
                            inc(g.dma_start(
                                out=acc[:, cch * HT:(cch + 1) * HT],
                                in_=u_t[cch * 128:(cch + 1) * 128,
                                        half * HT:(half + 1) * HT]), "dG", 16)
                    sect("gpsimd", f_g)
                dg_u = C["dG"]
                pe_base = C["pe"]
                cast_v = []

                def f_v(v):
                    v.wait_ge(sems["dG"], dg_u)
                    for cch in range(NKC):
                        sl = acc[:, cch * HT + ts0: cch * HT + ts0 + QT]
                        p = cch % 2
                        if cch >= 2:
                            v.wait_ge(sems["pe"], pe_base + (cch - 2) + 1)
                        v.tensor_copy(ubf[:, p * 512:(p + 1) * 512], sl)
                        ins = v.tensor_tensor(usq[:, p * 512:(p + 1) * 512], sl, sl,
                                              op=ALU.mult)
                        cast_v.append(inc(ins, "ve"))
                sect("vector", f_v)

                def f_t(t):
                    for cch in range(NKC):
                        p = cch % 2
                        t.wait_ge(sems["ve"], cast_v[cch])
                        t.matmul(p_s1[:, :], onescol[:, :], ubf[:, p * 512:(p + 1) * 512],
                                 start=(cch == 0), stop=(cch == NKC - 1))
                        ins = t.matmul(p_s2[:, :], onescol[:, :],
                                       usq[:, p * 512:(p + 1) * 512],
                                       start=(cch == 0), stop=(cch == NKC - 1))
                        inc(ins, "pe")
                sect("tensor", f_t)
                pe_mm = C["pe"]

                def f_v2(v):
                    v.wait_ge(sems["pe"], pe_mm)
                    m, vv, r = (ln_t[:, i * QT:(i + 1) * QT] for i in range(3))
                    v.tensor_scalar(m, p_s1[:, :], 1.0 / D, None, op0=ALU.mult)
                    v.tensor_scalar(vv, p_s2[:, :], 1.0 / D, None, op0=ALU.mult)
                    v.drain()
                    v.tensor_tensor(r, m, m, op=ALU.mult)
                    v.drain()
                    inc(v.tensor_tensor(vv, vv, r, op=ALU.subtract), "ve")
                sect("vector", f_v2)
                ve_st = C["ve"]

                def f_s(s):
                    s.wait_ge(sems["ve"], ve_st)
                    inc(s.activation(ln_t[:, 2 * QT:], ln_t[:, QT:2 * QT], AF.Sqrt,
                                     bias=cc[0:1, 1:2], scale=1.0), "ac")
                sect("scalar", f_s)
                ac_sq = C["ac"]

                def f_v3(v):
                    v.wait_ge(sems["ac"], ac_sq)
                    r, m = ln_t[:, 2 * QT:], ln_t[:, 0:QT]
                    v.reciprocal(r, r)
                    v.drain()
                    inc(v.tensor_tensor(m, m, r, op=ALU.mult), "ve")
                sect("vector", f_v3)
                ve_r = C["ve"]

                def f_t2(t):
                    t.wait_ge(sems["ve"], ve_r)
                    t.matmul(p_rb[:, :], onesrow[:, :], ln_t[0:1, 2 * QT:],
                             start=True, stop=True)
                    inc(t.matmul(p_mrb[:, :], onesrow[:, :], ln_t[0:1, 0:QT],
                                 start=True, stop=True), "pe")
                sect("tensor", f_t2)
                pe_bc = C["pe"]

                def f_v4(v):
                    v.wait_ge(sems["pe"], pe_bc)
                    for cch in range(NKC):
                        sl = acc[:, cch * HT + ts0: cch * HT + ts0 + QT]
                        xsl = xn[:, cch * HT + ts0: cch * HT + ts0 + QT]
                        v.tensor_tensor(tmp1024[:, 0:QT], sl, p_rb[:, :], op=ALU.mult)
                        v.drain()
                        note("vector", v.tensor_tensor(xsl, tmp1024[:, 0:QT],
                                                       p_mrb[:, :], op=ALU.subtract))
                        v.drain()
                sect("vector", f_v4)
                barrier()

        def z_phase(half, q):
            with (pt("p_z0", [128, 512], F32) as p_z0,
                  pt("p_z1", [128, 512], F32) as p_z1,
                  pt("p_cv0", [128, 512], F32) as p_cv0,
                  pt("p_cv1", [128, 512], F32) as p_cv1,
                  pt("p_dt", [128, 64], F32) as p_dt):
                p_z = [p_z0, p_z1]
                p_cv = [p_cv0, p_cv1]
                ts0 = q * QT
                xn_sl = lambda k: xn[:, k * HT + ts0: k * HT + ts0 + QT]

                dma_d, cast_a, mm_pe, evac_v, cv_pe, silu_a = [], [], [], [], [], []
                for m in range(NIC):
                    p = m % 2

                    def f_q(g, m=m, p=p):
                        if m >= 2:
                            g.wait_ge(sems["ve"], cast_a[m - 2])
                        else:
                            g.wait_ge(sems[state["stg_cast_last"][0]],
                                      state["stg_cast_last"][1])
                        ins = g.dma_start(out=stg[:, p * 2048: p * 2048 + 1536],
                                          in_=win_x[m, :, :])
                        dma_d.append(inc(ins, f"dW{p}", 16))
                    sect("sync", f_q)

                    def f_c(v, m=m, p=p):
                        v.wait_ge(sems[f"dW{p}"], dma_d[m])
                        if m >= 2:
                            v.wait_ge(sems["pe"], cv_pe[m - 2])
                        ins = v.tensor_copy(wbf[:, p * 2048: p * 2048 + 1536],
                                            stg[:, p * 2048: p * 2048 + 1536])
                        cast_a.append(inc(ins, "ve"))
                    sect("vector", f_c)

                    def f_t(t, m=m, p=p):
                        t.wait_ge(sems["ve"], cast_a[m])
                        if m >= 2:
                            t.wait_ge(sems["ve"], evac_v[m - 2])
                        for k in range(NKC):
                            ins = t.matmul(p_z[p][:, :],
                                           wbf[:, p * 2048 + k * 128:
                                               p * 2048 + (k + 1) * 128],
                                           xn_sl(k), start=(k == 0), stop=(k == NKC - 1))
                        mm_pe.append(inc(ins, "pe"))
                    sect("tensor", f_t)

                    def f_v(v, m=m, p=p):
                        v.wait_ge(sems["pe"], mm_pe[m])
                        v.tensor_copy(x_pre[:, 0:3], tails[:, m * 3:(m + 1) * 3])
                        v.tensor_copy(x_pre[:, 3:3 + QT], p_z[p][:, :])
                        v.drain()
                        ins = v.tensor_copy(tails[:, m * 3:(m + 1) * 3],
                                            x_pre[:, QT:QT + 3])
                        evac_v.append(inc(ins, "ve"))
                    sect("vector", f_v)

                    def f_t2(t, m=m, p=p):
                        t.wait_ge(sems["ve"], evac_v[m])
                        if m >= 2:
                            t.wait_ge(sems["ac"], silu_a[m - 2])
                        ins = None
                        for kk in range(DC):
                            ins = t.matmul(p_cv[p][:, :],
                                           wbf[:, p * 2048 + 1024 + kk * 128:
                                               p * 2048 + 1024 + (kk + 1) * 128],
                                           x_pre[:, kk:kk + QT],
                                           start=(kk == 0), stop=(kk == DC - 1))
                        cv_pe.append(inc(ins, "pe"))
                    sect("tensor", f_t2)

                    def f_s2(s, m=m, p=p):
                        s.wait_ge(sems["pe"], cv_pe[m])
                        ins = s.activation(A1[:, 24 * HT + m * QT: 24 * HT + (m + 1) * QT],
                                           p_cv[p][:, :],
                                           AF.Silu, bias=convb[:, m:m + 1])
                        silu_a.append(inc(ins, "ac"))
                        note("scalar", ins)
                    sect("scalar", f_s2)

                bc_dma, bc_cast, bc_mm, bc_ev = [], [], [], []
                for idx in range(NH):
                    isC = idx >= 8
                    j = idx % 8
                    src = win_c if isC else win_b
                    p = idx % 2

                    def f_q(g, idx=idx, j=j, src=src, p=p):
                        if idx >= 2:
                            g.wait_ge(sems["ve"], bc_cast[idx - 2])
                        else:
                            g.wait_ge(sems["ve"], cast_a[-1])
                        ins = g.dma_start(out=stg[:, p * 2048: p * 2048 + 1024],
                                          in_=src[j, :, :])
                        bc_dma.append(inc(ins, f"dW{p}", 16))
                    sect("sync", f_q)

                    def f_c(v, idx=idx, p=p):
                        v.wait_ge(sems[f"dW{p}"], bc_dma[idx])
                        if idx >= 2:
                            v.wait_ge(sems["pe"], bc_mm[idx - 2])
                        else:
                            v.wait_ge(sems["pe"], mm_pe[-1])
                        ins = v.tensor_copy(wbf[:, p * 2048: p * 2048 + 1024],
                                            stg[:, p * 2048: p * 2048 + 1024])
                        bc_cast.append(inc(ins, "ve"))
                    sect("vector", f_c)

                    def f_t(t, idx=idx, p=p):
                        t.wait_ge(sems["ve"], bc_cast[idx])
                        if idx >= 2:
                            t.wait_ge(sems["ve"], bc_ev[idx - 2])
                        else:
                            t.wait_ge(sems["ve"], evac_v[-1])
                        ins = None
                        for k in range(NKC):
                            ins = t.matmul(
                                p_z[p][:, :],
                                wbf[:, p * 2048 + k * 128: p * 2048 + (k + 1) * 128],
                                xn_sl(k), start=(k == 0), stop=(k == NKC - 1))
                        bc_mm.append(inc(ins, "pe"))
                    sect("tensor", f_t)

                    def f_v(v, idx=idx, j=j, isC=isC, p=p):
                        v.wait_ge(sems["pe"], bc_mm[idx])
                        base = (8 * QT if isC else 0) + j * QT
                        ins = v.tensor_copy(A1[:, base: base + QT], p_z[p][:, :])
                        bc_ev.append(inc(ins, "ve"))
                    sect("vector", f_v)

                dt_pe, dt_ve = [], []
                for tc in range(QT // 128):
                    def f_t2(t, tc=tc):
                        if tc >= 1:
                            t.wait_ge(sems["ve"], dt_ve[tc - 1])
                        ins = None
                        for k in range(NKC):
                            ins = t.matmul(
                                p_dt[:, 0:16],
                                xn[:, k * HT + ts0 + tc * 128:
                                   k * HT + ts0 + (tc + 1) * 128],
                                wdt_bf[:, k * 16:(k + 1) * 16],
                                start=(k == 0), stop=(k == NKC - 1))
                        dt_pe.append(inc(ins, "pe"))
                    sect("tensor", f_t2)

                    def f_v2a(v, tc=tc):
                        v.wait_ge(sems["pe"], dt_pe[tc])
                        ins = v.tensor_copy(dtw[:, tc * 16:(tc + 1) * 16], p_dt[:, 0:16])
                        dt_ve.append(inc(ins, "ve"))
                    sect("vector", f_v2a)

                sl_ac = silu_a[-1]

                def f_v2(v):
                    v.wait_ge(sems["ve"], dt_ve[-1])
                    v.wait_ge(sems["ac"], sl_ac)
                    d0, d1 = dtw[:, 0:64], dtw[:, 64:128]
                    t_mx = tmp1024[:, 64:128]
                    v.drain()
                    v.tensor_tensor(d1, d0, dptile[:, :], op=ALU.add)
                    v.drain()
                    inc(v.tensor_scalar(t_mx, d1, 0.0, None, op0=ALU.max), "ve")
                sect("vector", f_v2)
                ve_dt = C["ve"]

                def f_s3(s):
                    s.wait_ge(sems["ve"], ve_dt)
                    d0, d1, d2 = (dtw[:, i * 64:(i + 1) * 64] for i in range(3))
                    t_ax = tmp1024[:, 0:64]
                    s.activation(d2, d0, AF.Abs)
                    s.activation(t_ax, d1, AF.Abs)
                    s.drain()
                    s.activation(t_ax, t_ax, AF.Exp, scale=-1.0,
                                 bias=cc[0:128, 3:4])
                    s.drain()
                    inc(s.activation(t_ax, t_ax, AF.Ln, bias=cc[0:128, 0:1]), "ac")
                sect("scalar", f_s3)
                ac_sp = C["ac"]

                def f_v3(v):
                    v.wait_ge(sems["ac"], ac_sp)
                    d1, d2, d3 = (dtw[:, i * 64:(i + 1) * 64] for i in range(1, 4))
                    t_ax = tmp1024[:, 0:64]
                    t_mx = tmp1024[:, 64:128]
                    v.tensor_tensor(d1, t_ax, t_mx, op=ALU.add)
                    v.drain()
                    note("vector", v.tensor_tensor(d3, d1, d2, op=ALU.mult))
                sect("vector", f_v3)
                state["stg_cast_last"] = ("ve", bc_cast[-1])
                barrier()

        def gate_phase(half, q):
            with (pt("p_z0", [128, 512], F32) as p_z0,
                  pt("p_z1", [128, 512], F32) as p_z1):
                p_z = [p_z0, p_z1]
                ts0 = q * QT
                xn_sl = lambda k: xn[:, k * HT + ts0: k * HT + ts0 + QT]
                g_dma, g_cast, g_mm, g_ev = [], [], [], []
                for m in range(NIC):
                    p = m % 2

                    def f_q(g, m=m, p=p):
                        if m >= 2:
                            g.wait_ge(sems["ve"], g_cast[m - 2])
                        else:
                            g.wait_ge(sems[state["stg_cast_last"][0]], state["stg_cast_last"][1])
                        ins = g.dma_start(out=stg[:, p * 2048: p * 2048 + 1024],
                                          in_=wgate[m, :, :])
                        g_dma.append(inc(ins, f"dW{p}", 16))
                    sect("sync", f_q)

                    def f_c(v, m=m, p=p):
                        v.wait_ge(sems[f"dW{p}"], g_dma[m])
                        if m >= 2:
                            v.wait_ge(sems["pe"], g_mm[m - 2])
                        ins = v.tensor_copy(wbf[:, p * 2048: p * 2048 + 1024],
                                            stg[:, p * 2048: p * 2048 + 1024])
                        g_cast.append(inc(ins, "ve"))
                    sect("vector", f_c)

                    def f_t(t, m=m, p=p):
                        t.wait_ge(sems["ve"], g_cast[m])
                        if m >= 2:
                            t.wait_ge(sems["ac"], g_ev[m - 2])
                        ins = None
                        for k in range(NKC):
                            ins = t.matmul(p_z[p][:, :],
                                           wbf[:, p * 2048 + k * 128:
                                               p * 2048 + (k + 1) * 128],
                                           xn_sl(k), start=(k == 0), stop=(k == NKC - 1))
                        g_mm.append(inc(ins, "pe"))
                    sect("tensor", f_t)

                    def f_s2(s, m=m, p=p):
                        s.wait_ge(sems["pe"], g_mm[m])
                        ins = s.activation(A1[:, (16 + m) * QT:(17 + m) * QT],
                                           p_z[p][:, :], AF.Sigmoid)
                        g_ev.append(inc(ins, "ac"))
                        note("scalar", ins)
                    sect("scalar", f_s2)
                state["stg_cast_last"] = ("ve", g_cast[-1])
                barrier()

        def ssd_phase(half, q):
            with (pt("pY0", [128, 128], F32) as pY0,
                  pt("pY1", [128, 128], F32) as pY1,
                  pt("pCB0", [128, 128], F32) as pCB0,
                  pt("pCB1", [128, 128], F32) as pCB1,
                  pt("p_sb", [128, 512], F32) as p_sb,
                  pt("pXt", [128, 128], BF16) as pXt,
                  pt("pBt", [128, 128], BF16) as pBt,
                  pt("pdH", [128, 128], F32) as pdH):
                pYs = [pY0, pY1]
                pCBs = [pCB0, pCB1]
                negla4 = dtw[:, 192:256]

                def f_t1(t, w=state["ssd_prev_yg"]):
                    t.wait_ge(sems["ve"], w)
                    inc(t.matmul(pY0[:, 0:64], tri01[:, :], negla4,
                                 start=True, stop=True), "pe")
                sect("tensor", f_t1)
                pe_c1 = C["pe"]

                def f_v1(v, wac=state["ssd_prev_ac"]):
                    v.wait_ge(sems["pe"], pe_c1)
                    v.wait_ge(sems["ac"], wac)
                    inc(v.tensor_copy(nsc_tok[:, :], pY0[:, 0:64]), "ve")
                sect("vector", f_v1)
                ve_d1 = C["ve"]

                def f_t2(t):
                    t.wait_ge(sems["ve"], ve_d1)
                    inc(t.transpose(pY1[0:64, :], nsc_tok[:, :], idf[:, :]), "pe")
                sect("tensor", f_t2)
                pe_c2 = C["pe"]

                def f_v2(v, wac=state["ssd_prev_ac"]):
                    v.wait_ge(sems["pe"], pe_c2)
                    v.wait_ge(sems["ac"], wac)
                    v.tensor_copy(nsc_hm[:, :], pY1[0:64, :])
                    v.drain()
                    inc(v.tensor_scalar(nb16[:, :], nsc_hm[:, 127:128], -1.0, None,
                                        op0=ALU.mult), "ve")
                sect("vector", f_v2)
                ve_d2 = C["ve"]

                def f_s1(s):
                    s.wait_ge(sems["ve"], ve_d2)
                    inc(s.activation(w_hm[:, :], nsc_hm[:, :], AF.Exp,
                                     bias=nb16[:, :], scale=1.0), "ac")
                sect("scalar", f_s1)
                ac_w = C["ac"]

                def f_t3(t):
                    t.wait_ge(sems["ac"], ac_w)
                    inc(t.transpose(pCB0[:, 0:64], w_hm[:, :], idf[0:64, 0:64]), "pe")
                sect("tensor", f_t3)
                pe_c4 = C["pe"]

                def f_v3(v):
                    v.wait_ge(sems["pe"], pe_c4)
                    inc(v.tensor_copy(w_tok[:, :], pCB0[:, 0:64]), "ve")
                sect("vector", f_v3)
                ve_d4q = C["ve"]
                state["ssd_prev_d4"] = ve_d4q

                e_ac_prev_chunk = state["ssd_prev_ac"]
                pv_xtok = ve_d4q
                pv_wb_free = ve_d4q
                pv_hup = ve_d4q
                ve_wb = ve_d4q
                prev_mp_last = ve_d4q
                prev_pe_sb = C["pe"]
                hist_mp = [ve_d4q, ve_d4q]
                hist_yg = [ve_d1, ve_d2]
                for kq in range(QT // Q):
                    pe_sb, sb_ve, sb_ac = [], [], []
                    mp_hist = []
                    for qq in range(4):
                        def f_g(g, kq=kq, qq=qq, prev_pe_sb=prev_pe_sb):
                            g.wait_ge(sems["ve"], ve_d2)
                            if qq >= 1:
                                g.wait_ge(sems["pe"], pe_sb[qq - 1])
                            else:
                                g.wait_ge(sems["pe"], prev_pe_sb)
                            inc(g.dma_start(
                                out=rowbuf[0:1, 0:512],
                                in_=nsc_hm[kq * 16 + qq * 4: kq * 16 + (qq + 1) * 4, :]),
                                "dG", 16)
                        sect("gpsimd", f_g)
                        dg_f = C["dG"]

                        def f_t4(t, qq=qq, dg_f=dg_f):
                            t.wait_ge(sems["dG"], dg_f)
                            if qq >= 1:
                                t.wait_ge(sems["ve"], sb_ve[qq - 1])
                                t.wait_ge(sems["ac"], sb_ac[qq - 1])
                            ins = t.matmul(p_sb[:, :], onesrow[:, :],
                                           rowbuf[0:1, 0:512],
                                           start=True, stop=True)
                            pe_sb.append(inc(ins, "pe"))
                        sect("tensor", f_t4)

                        def f_v4(v, kq=kq, qq=qq, wac=e_ac_prev_chunk):
                            v.wait_ge(sems["pe"], pe_sb[qq])
                            v.wait_ge(sems["ac"], wac)
                            v.scalar_tensor_tensor(
                                SBm[:, qq * 512:(qq + 1) * 512], p_sb[:, :], 1.0,
                                trimask[:, :].rearrange("p (a c) -> p a c", a=1)
                                .broadcast_to([128, 4, 128]),
                                op0=ALU.mult, op1=ALU.add)
                            v.tensor_scalar(alph[:, qq * 4:(qq + 1) * 4],
                                            p_sb[:, 127:512:128], 1.0, None,
                                            op0=ALU.mult)
                            v.drain()
                            ins = None
                            for hh in range(4):
                                h = qq * 4 + hh
                                ins = v.tensor_scalar(
                                    SBm[:, h * 128:(h + 1) * 128],
                                    SBm[:, h * 128:(h + 1) * 128],
                                    nsc_tok[:, kq * 16 + h: kq * 16 + h + 1], None,
                                    op0=ALU.subtract)
                            sb_ve.append(inc(ins, "ve"))
                        sect("vector", f_v4)

                        def f_s2(s, qq=qq, wve=state["ssd_prev_yg"], kq=kq,
                                 prev_mp_last=prev_mp_last):
                            s.wait_ge(sems["pe"], pe_sb[qq])
                            s.wait_ge(sems["ve"], wve)
                            s.wait_ge(sems["ve"], sb_ve[qq])
                            s.activation(expSB[:, qq * 512:(qq + 1) * 512],
                                         p_sb[:, :], AF.Exp,
                                         bias=cc[0:128, 3:4], scale=-1.0)
                            s.drain()
                            if qq >= 1:
                                s.wait_ge(sems["ve"], mp_hist[-1])
                            else:
                                s.wait_ge(sems["ve"], prev_mp_last)
                            s.activation(alph[:, qq * 4:(qq + 1) * 4],
                                         alph[:, qq * 4:(qq + 1) * 4], AF.Exp,
                                         bias=cc[0:128, 3:4], scale=-1.0)
                            s.drain()
                            ins = s.activation(t_e[:, :],
                                               SBm[:, qq * 512:(qq + 1) * 512],
                                               AF.Exp, bias=cc[0:128, 3:4],
                                               scale=-1.0)
                            sb_ac.append(inc(ins, "ac"))
                        sect("scalar", f_s2)
                        e_ready = C["ac"]

                        for hh in range(4):
                            h = qq * 4 + hh
                            base = (h % 2) * 64
                            jj = h // 2
                            x_sl = A1[:, 24 * HT + h * QT + kq * 128:
                                      24 * HT + h * QT + (kq + 1) * 128]
                            bcols = slice(jj * QT + kq * 128, jj * QT + (kq + 1) * 128)
                            ccols = slice(8 * QT + jj * QT + kq * 128,
                                          8 * QT + jj * QT + (kq + 1) * 128)
                            gcols = slice((16 + h) * QT + kq * 128,
                                          (16 + h) * QT + (kq + 1) * 128)

                            def f_t5(t, w=pv_xtok, x_sl=x_sl):
                                t.wait_ge(sems["ve"], w)
                                inc(t.transpose(pXt[:, :], x_sl, idbf[:, :]), "pe")
                            sect("tensor", f_t5)
                            pe_tx = C["pe"]

                            def f_v5(v, pe_tx=pe_tx, h=h):
                                v.wait_ge(sems["pe"], pe_tx)
                                slm = (h % 2) * 128
                                inc(v.tensor_copy(t_xtok[:, slm:slm + 128],
                                                  pXt[:, :]), "ve")
                            sect("vector", f_v5)
                            ve_xtok = C["ve"]

                            if h % 2 == 0:
                                def f_t6(t, w=pv_wb_free, bcols=bcols):
                                    t.wait_ge(sems["ve"], w)
                                    inc(t.transpose(pBt[:, :], A1[:, bcols],
                                                    idbf[:, :]), "pe")
                                sect("tensor", f_t6)
                                pe_tb = C["pe"]

                                def f_v6(v, pe_tb=pe_tb, h=h, kq=kq, jj=jj):
                                    v.wait_ge(sems["pe"], pe_tb)
                                    wc = kq * 16 + h
                                    sw = (jj % 2) * 128
                                    v.tensor_scalar(t_wb[:, sw:sw + 64], pBt[:, 0:64],
                                                    w_tok[:, wc:wc + 1], None,
                                                    op0=ALU.mult)
                                    inc(v.tensor_scalar(t_wb[:, sw + 64:sw + 128],
                                                        pBt[:, 64:128],
                                                        w_tok[:, wc + 1:wc + 2], None,
                                                        op0=ALU.mult), "ve")
                                sect("vector", f_v6)
                                ve_wb = C["ve"]

                            def f_t7(t, w=hist_mp[h % 2], bcols=bcols,
                                     ccols=ccols, base=base, h=h):
                                t.wait_ge(sems["ve"], w)
                                inc(t.matmul(pCBs[h % 2][:, :],
                                             A1[base:base + 64, bcols],
                                             A1[base:base + 64, ccols],
                                             start=True, stop=True), "pe")
                            sect("tensor", f_t7)
                            pe_cb = C["pe"]

                            def f_v7(v, pe_cb=pe_cb, ac_e=e_ready, hh=hh, h=h):
                                v.wait_ge(sems["pe"], pe_cb)
                                v.wait_ge(sems["ac"], ac_e)
                                inc(v.tensor_tensor(
                                    t_mp[:, (h % 2) * 128:(h % 2 + 1) * 128],
                                    pCBs[h % 2][:, :],
                                    t_e[:, hh * 128:(hh + 1) * 128],
                                    op=ALU.mult), "ve")
                            sect("vector", f_v7)
                            ve_mp = C["ve"]
                            mp_hist.append(ve_mp)
                            hist_mp[h % 2] = ve_mp

                            def f_v8(v, ccols=ccols, base=base, h=h, jj=jj,
                                     w=sb_ac[-1]):
                                sl2 = (jj % 2) * 128
                                v.wait_ge(sems["ac"], w)
                                v.tensor_tensor(t_cs[base:base + 64, sl2:sl2 + 128],
                                                A1[base:base + 64, ccols],
                                                expSB[base:base + 64,
                                                      h * 128: h * 128 + 128],
                                                op=ALU.mult)
                                inc(v.tensor_copy(t_hbf[base:base + 64, sl2:sl2 + 128],
                                                  Hp[base:base + 64,
                                                     jj * 128:(jj + 1) * 128]), "ve")
                            sect("vector", f_v8)
                            ve_cs = C["ve"]

                            def f_t8(t, wb=ve_wb, xt=ve_xtok, hu=pv_hup,
                                     base=base, h=h, jj=jj):
                                sw = (jj % 2) * 128
                                slm = (h % 2) * 128
                                t.wait_ge(sems["ve"], wb)
                                t.wait_ge(sems["ve"], xt)
                                t.wait_ge(sems["ve"], hu)
                                inc(t.matmul(pdH[base:base + 64, :],
                                             t_wb[:, sw + base:sw + base + 64],
                                             t_xtok[:, slm:slm + 128],
                                             start=True, stop=True,
                                             tile_position=(0, base)), "pe")
                            sect("tensor", f_t8)
                            pe_dh = C["pe"]

                            def f_t9(t, mp=ve_mp, cs=ve_cs, yg=hist_yg[h % 2],
                                     base=base, h=h, jj=jj):
                                sl2 = (jj % 2) * 128
                                slm = (h % 2) * 128
                                t.wait_ge(sems["ve"], mp)
                                t.wait_ge(sems["ve"], cs)
                                t.wait_ge(sems["ve"], yg)
                                t.matmul(pYs[h % 2][:, :],
                                         t_xtok[:, slm:slm + 128],
                                         t_mp[:, slm:slm + 128],
                                         start=True, stop=False)
                                inc(t.matmul(pYs[h % 2][:, :],
                                             t_hbf[base:base + 64, sl2:sl2 + 128],
                                             t_cs[base:base + 64, sl2:sl2 + 128],
                                             start=False, stop=True), "pe")
                            sect("tensor", f_t9)
                            pe_y = C["pe"]

                            def f_v9(v, pe_dh=pe_dh, pe_y=pe_y, base=base, h=h,
                                     jj=jj, gcols=gcols, wal=sb_ac[-1]):
                                v.wait_ge(sems["pe"], pe_dh)
                                v.wait_ge(sems["ac"], wal)
                                v.drain()
                                alpha = alph[base:base + 64, h:h + 1]
                                inc(v.scalar_tensor_tensor(
                                    Hp[base:base + 64, jj * 128:(jj + 1) * 128],
                                    Hp[base:base + 64, jj * 128:(jj + 1) * 128],
                                    alpha, pdH[base:base + 64, :],
                                    op0=ALU.mult, op1=ALU.add), "ve")
                                v.wait_ge(sems["pe"], pe_y)
                                v.drain()
                                ins = v.tensor_tensor(A1[:, gcols], pYs[h % 2][:, :],
                                                      A1[:, gcols], op=ALU.mult)
                                inc(ins, "ve")
                                note("vector", ins)
                            sect("vector", f_v9)
                            pv_hup = C["ve"] - 1
                            hist_yg[h % 2] = C["ve"]
                            pv_xtok = ve_xtok
                            pv_wb_free = ve_wb
                    e_ac_prev_chunk = sb_ac[-1]
                    prev_mp_last = mp_hist[-1]
                    prev_pe_sb = pe_sb[-1]
                    state["ssd_prev_yg"] = max(hist_yg)
                state["ssd_prev_ac"] = C["ac"]
                barrier()

        def rms_wout_phase(half, q):
            with (pt("p_s1", [1, 512], F32) as p_s1,
                  pt("p_rb", [128, 512], F32) as p_rb,
                  pt("p_w0", [128, 512], F32) as p_w0,
                  pt("p_w1", [128, 512], F32) as p_w1):
                p_w = [p_w0, p_w1]
                ts0 = q * QT
                pe_rms = []
                for h in range(NIC):
                    def f_v1(v, h=h):
                        if h > 0:
                            v.wait_ge(sems["pe"], pe_rms[h - 1])
                        gsl = A1[:, (16 + h) * QT:(17 + h) * QT]
                        inc(v.tensor_tensor(ysq_t[:, :], gsl, gsl, op=ALU.mult), "ve")
                    sect("vector", f_v1)
                    ve_y = C["ve"]

                    def f_t1(t, h=h, ve_y=ve_y):
                        t.wait_ge(sems["ve"], ve_y)
                        ins = t.matmul(p_s1[:, :], onescol[:, :], ysq_t[:, :],
                                       start=(h == 0), stop=(h == NIC - 1))
                        pe_rms.append(inc(ins, "pe"))
                    sect("tensor", f_t1)

                def f_s(s):
                    s.wait_ge(sems["pe"], pe_rms[-1])
                    inc(s.activation(rms_t[:, :], p_s1[:, :], AF.Sqrt,
                                     bias=cc[0:1, 2:3], scale=1.0 / DI), "ac")
                sect("scalar", f_s)
                ac_r = C["ac"]

                def f_v2(v):
                    v.wait_ge(sems["ac"], ac_r)
                    inc(v.reciprocal(rms_t[:, :], rms_t[:, :]), "ve")
                sect("vector", f_v2)
                ve_rr = C["ve"]

                def f_t2(t):
                    t.wait_ge(sems["ve"], ve_rr)
                    inc(t.matmul(p_rb[:, :], onesrow[:, :], rms_t[0:1, :],
                                 start=True, stop=True), "pe")
                sect("tensor", f_t2)
                pe_rb2 = C["pe"]

                yn_ve = []

                def f_v3(v):
                    v.wait_ge(sems["pe"], pe_rb2)
                    for h in range(NIC):
                        gsl = A1[:, (16 + h) * QT:(17 + h) * QT]
                        ins = v.scalar_tensor_tensor(gsl, gsl, normw[:, h:h + 1],
                                                     p_rb[:, :], op0=ALU.mult,
                                                     op1=ALU.mult)
                        yn_ve.append(inc(ins, "ve"))
                        v.drain()
                sect("vector", f_v3)

                wo_dma, wo_cast, wo_mm, wo_ev = [], [], [], []
                for o in range(NKC):
                    p = o % 2

                    def f_q(g, o=o, p=p):
                        if o >= 2:
                            g.wait_ge(sems["ve"], wo_cast[o - 2])
                        else:
                            g.wait_ge(sems[state["stg_cast_last"][0]], state["stg_cast_last"][1])
                        ins = g.dma_start(out=stg[:, p * 2048:(p + 1) * 2048],
                                          in_=wout[o, :, :])
                        wo_dma.append(inc(ins, f"dW{p}", 16))
                    sect("sync", f_q)

                    def f_c(v, o=o, p=p):
                        v.wait_ge(sems[f"dW{p}"], wo_dma[o])
                        if o >= 2:
                            v.wait_ge(sems["pe"], wo_mm[o - 2])
                        ins = v.tensor_copy(wbf[:, p * 2048:(p + 1) * 2048],
                                            stg[:, p * 2048:(p + 1) * 2048])
                        wo_cast.append(inc(ins, "ve"))
                    sect("vector", f_c)

                    def f_t(t, o=o, p=p):
                        t.wait_ge(sems["ve"], wo_cast[o])
                        t.wait_ge(sems["ve"], yn_ve[-1])
                        if o >= 2:
                            t.wait_ge(sems["ve"], wo_ev[o - 2])
                        ins = None
                        for kk in range(NIC):
                            ins = t.matmul(p_w[p][:, :],
                                           wbf[:, p * 2048 + kk * 128:
                                               p * 2048 + (kk + 1) * 128],
                                           A1[:, (16 + kk) * QT:(17 + kk) * QT],
                                           start=(kk == 0), stop=(kk == NIC - 1))
                        wo_mm.append(inc(ins, "pe"))
                    sect("tensor", f_t)

                    def f_v4(v, o=o, p=p):
                        v.wait_ge(sems["pe"], wo_mm[o])
                        asl = acc[:, o * HT + ts0: o * HT + ts0 + QT]
                        ins = v.tensor_tensor(asl, p_w[p][:, :], asl, op=ALU.add)
                        wo_ev.append(inc(ins, "ve"))
                        v.drain()
                        note("vector", ins)
                    sect("vector", f_v4)
                state["stg_cast_last"] = ("ve", wo_cast[-1])
                barrier()

        def router_phase(half):
            with (pt("p_lg", [8, 1024], F32) as p_lg,
                  pt("p_tok0", [128, 8], F32) as p_tok0,
                  pt("p_tok1", [128, 8], F32) as p_tok1,
                  pt("p_st0", [8, 128], F32) as p_st0,
                  pt("p_st1", [8, 128], F32) as p_st1,
                  pt("p_sb", [128, 512], F32) as p_sb,
                  pt("p_sb2", [128, 512], F32) as p_sb2):
                p_tok = [p_tok0, p_tok1]
                p_st = [p_st0, p_st1]

                def f_t(t):
                    ins = None
                    for tsl in range(2):
                        for k in range(NKC):
                            ins = t.matmul(p_lg[:, tsl * 512:(tsl + 1) * 512],
                                           wrout_bf[:, k * 8:(k + 1) * 8],
                                           xn[:, k * HT + tsl * 512:
                                              k * HT + (tsl + 1) * 512],
                                           start=(k == 0), stop=(k == NKC - 1))
                    inc(ins, "pe")
                sect("tensor", f_t)
                pe_lgv = C["pe"]

                def f_v(v):
                    v.wait_ge(sems["pe"], pe_lgv)
                    inc(v.tensor_scalar(lt_r[:, :], p_lg[:, :], routb[:, :], None,
                                        op0=ALU.add), "ve")
                sect("vector", f_v)
                ve_lt = C["ve"]

                exp_ac, st_ve = [], []
                for tc in range(HT // 128):
                    p = tc % 2

                    def f_t1(t, tc=tc, p=p):
                        t.wait_ge(sems["ve"], ve_lt)
                        if tc >= 2:
                            t.wait_ge(sems["ac"], exp_ac[tc - 2])
                        inc(t.transpose(p_tok[p][:, :],
                                        lt_r[:, tc * 128:(tc + 1) * 128],
                                        idf[0:8, 0:8]), "pe")
                    sect("tensor", f_t1)
                    pe_tt = C["pe"]

                    def f_v1(v, pe_tt=pe_tt, p=p):
                        v.wait_ge(sems["pe"], pe_tt)
                        inc(v.tensor_reduce(r_sc[:, 0:1], p_tok[p][:, :], axis=AXX,
                                            op=ALU.max, negate=True), "ve")
                    sect("vector", f_v1)
                    ve_nm = C["ve"]

                    def f_s1(s, ve_nm=ve_nm, p=p):
                        s.wait_ge(sems["ve"], ve_nm)
                        ins = s.activation(e_t[:, :], p_tok[p][:, :], AF.Exp,
                                           bias=r_sc[:, 0:1], scale=1.0)
                        exp_ac.append(inc(ins, "ac"))
                    sect("scalar", f_s1)
                    ac_ex = C["ac"]

                    def f_v2(v, ac_ex=ac_ex):
                        v.wait_ge(sems["ac"], ac_ex)
                        v.tensor_reduce(r_sc[:, 1:2], e_t[:, :], axis=AXX, op=ALU.add)
                        v.drain()
                        v.reciprocal(r_sc[:, 2:3], r_sc[:, 1:2])
                        v.tensor_scalar(mask_t[:, :], e_t[:, :], 1.0, None,
                                        op0=ALU.is_equal)
                        v.drain()
                        inc(v.tensor_scalar(mask_t[:, :], mask_t[:, :], r_sc[:, 2:3],
                                            None, op0=ALU.mult), "ve")
                    sect("vector", f_v2)
                    ve_msk = C["ve"]

                    def f_t2(t, ve_msk=ve_msk, tc=tc, p=p):
                        t.wait_ge(sems["ve"], ve_msk)
                        if tc >= 2:
                            t.wait_ge(sems["ve"], st_ve[tc - 2])
                        inc(t.transpose(p_st[p][:, :], mask_t[:, :], idf[:, :]), "pe")
                    sect("tensor", f_t2)
                    pe_st2 = C["pe"]

                    def f_v3(v, pe_st2=pe_st2, tc=tc, p=p):
                        v.wait_ge(sems["pe"], pe_st2)
                        ins = v.tensor_copy(s_T[0:8, tc * 128:(tc + 1) * 128],
                                            p_st[p][:, :])
                        st_ve.append(inc(ins, "ve"))
                    sect("vector", f_v3)

                state["s_T_ready"] = st_ve[-1]
                barrier()

        def ffn_phase(half):
            for e in range(NE):
                def f_vb(v, e=e):
                    v.wait_ge(sems["ve"], state["s_T_ready"])
                    if state["shtmp_guard"]:
                        v.wait_ge(sems["dG"], state["shtmp_guard"])
                    ins = v.stream_shuffle(shtmp[:, :], s_T[:, :], [e] * 32)
                    inc(ins, "ve")
                sect("vector", f_vb)
                sh_rdy = C["ve"]

                def f_gb(g, e=e, sh_rdy=sh_rdy):
                    g.wait_ge(sems["ve"], sh_rdy)
                    if state["sb2_guard"][e % 2]:
                        g.wait_ge(sems["ve"], state["sb2_guard"][e % 2])
                    ins = g.dma_start(
                        out=s_b2[0:128:32, (e % 2) * HT:(e % 2 + 1) * HT],
                        in_=shtmp[0:4, :])
                    state["shtmp_guard"] = inc(ins, "dG", 16)
                sect("gpsimd", f_gb)
                dg_b = C["dG"]

                def f_vb2(v, e=e, dg_b=dg_b):
                    v.wait_ge(sems["dG"], dg_b)
                    sl = s_b2[:, (e % 2) * HT:(e % 2 + 1) * HT]
                    ins = v.stream_shuffle(sl, sl, [0] * 32)
                    inc(ins, "ve")
                sect("vector", f_vb2)
                sb2_rdy = C["ve"]
                with (pt("p_fa0", [128, 1024], F32) as p_fa0,
                      pt("p_fg0", [128, 1024], F32) as p_fg0,
                      pt("p_fa1", [128, 1024], F32) as p_fa1,
                      pt("p_fg1", [128, 1024], F32) as p_fg1):
                    p_fa = [p_fa0, p_fa1]
                    p_fg = [p_fg0, p_fg1]
                    pref = state.get("ffn_pref")
                    if pref is not None:
                        w1_dma = list(pref[0])
                        w1_cast = list(pref[1])
                        state["ffn_pref"] = None
                    else:
                        w1_dma, w1_cast = [], []
                    w1_amm, w1_gmm, w1_hg, w1_silu = [], [], [], []

                    def emit_hg(jh):
                        ph = jh % 2

                        def f_v(v, jh=jh, ph=ph):
                            v.wait_ge(sems["ac"], w1_silu[jh])
                            v.wait_ge(sems["pe"], w1_gmm[jh])
                            ins = v.tensor_tensor(
                                A1[:, jh * HT:(jh + 1) * HT],
                                wbf[:, ph * 2048: ph * 2048 + 1024],
                                p_fg[ph][:, :], op=ALU.mult)
                            w1_hg.append(inc(ins, "ve"))
                        sect("vector", f_v)

                    for j in range(32):
                        p = j % 2
                        if j < len(w1_dma):
                            pass
                        else:
                            def f_q(g, j=j, p=p, e=e):
                                if j >= 2:
                                    g.wait_ge(sems["ve"], w1_cast[j - 2])
                                else:
                                    g.wait_ge(sems[state["stg_cast_last"][0]],
                                              state["stg_cast_last"][1])
                                ins = g.dma_start(
                                    out=stg[:, p * 2048:(p + 1) * 2048],
                                    in_=w1t[e, :, j * 2048:(j + 1) * 2048])
                                w1_dma.append(inc(ins, f"dW{p}", 16))
                            sect("sync", f_q)

                            def f_cv(v, j=j, p=p, phg=state["prev_ffn_hg"]):
                                v.wait_ge(sems[f"dW{p}"], w1_dma[j])
                                if j >= 2:
                                    v.wait_ge(sems["ve"], w1_hg[j - 2])
                                elif phg:
                                    v.wait_ge(sems["ve"], phg)
                                ins = v.tensor_copy(wbf[:, p * 2048:(p + 1) * 2048],
                                                    stg[:, p * 2048:(p + 1) * 2048])
                                w1_cast.append(inc(ins, "ve"))
                            sect("vector", f_cv)

                        def f_t(t, j=j, p=p, pev=tuple(state["prev_ffn_ev_pair"])):
                            t.wait_ge(sems["ve"], w1_cast[j])
                            if j >= 2:
                                t.wait_ge(sems["ac"], w1_silu[j - 2])
                                t.wait_ge(sems["ve"], w1_hg[j - 2])
                            elif pev[p]:
                                t.wait_ge(sems["ve"], pev[p])
                            ins = None
                            for k in range(NKC):
                                for tsl in range(2):
                                    ins = t.matmul(
                                        p_fa[p][:, tsl * 512:(tsl + 1) * 512],
                                        wbf[:, p * 2048 + k * 128:
                                            p * 2048 + (k + 1) * 128],
                                        xn[:, k * HT + tsl * 512:
                                           k * HT + (tsl + 1) * 512],
                                        start=(k == 0), stop=(k == NKC - 1))
                            w1_amm.append(inc(ins, "pe"))
                            for k in range(NKC):
                                for tsl in range(2):
                                    ins = t.matmul(
                                        p_fg[p][:, tsl * 512:(tsl + 1) * 512],
                                        wbf[:, p * 2048 + 1024 + k * 128:
                                            p * 2048 + 1024 + (k + 1) * 128],
                                        xn[:, k * HT + tsl * 512:
                                           k * HT + (tsl + 1) * 512],
                                        start=(k == 0), stop=(k == NKC - 1))
                            w1_gmm.append(inc(ins, "pe"))
                        sect("tensor", f_t)

                        def f_s2(s, j=j, p=p):
                            s.wait_ge(sems["pe"], w1_amm[j])
                            ins = s.activation(wbf[:, p * 2048: p * 2048 + 1024],
                                               p_fa[p][:, :], AF.Silu)
                            w1_silu.append(inc(ins, "ac"))
                        sect("scalar", f_s2)

                        if j >= 1:
                            emit_hg(j - 1)
                    emit_hg(31)

                with contextlib.ExitStack() as fs:
                    p_fo = [fs.enter_context(
                        pt(f"p_fo{i}", [128, 512], F32)) for i in range(8)]
                    ev_done = []
                    prev_cast = None
                    pend_w2 = state.get("w2_pend")
                    state["w2_pend"] = None
                    for og in range(4):
                        gp_par = og % 2
                        if og == 0 and pend_w2 is not None:
                            w2_dma = list(pend_w2[0])
                            w2_cast = list(pend_w2[1])
                        elif og > 0 and state.get("w2_pend_next"):
                            w2_dma = list(state["w2_pend_next"][0])
                            w2_cast = list(state["w2_pend_next"][1])
                            state["w2_pend_next"] = None
                        else:
                            w2_dma, w2_cast = [], []
                        w2_mm = []

                        def emit_w2_dc(j, tgt_e, tgt_og, dma_l, cast_l,
                                       dma_guard, cast_guard):
                            s4 = j % 4

                            def f_q(g, j=j, s4=s4, dg=dma_guard):
                                if dg:
                                    g.wait_ge(sems["ve"], dg)
                                ins = g.dma_start(
                                    out=stgw2[:, s4 * 256:(s4 + 1) * 256],
                                    in_=w2t[tgt_e, tgt_og, :,
                                            j * 256:(j + 1) * 256])
                                dma_l.append(inc(ins, f"dW{2 + s4}", 16))
                            sect("sync", f_q)

                            def f_c(v, j=j, s4=s4, cg=cast_guard):
                                v.wait_ge(sems[f"dW{2 + s4}"], dma_l[j])
                                if cg:
                                    v.wait_ge(sems["pe"], cg)
                                ins = v.tensor_copy(
                                    w2bf[:, s4 * 256:(s4 + 1) * 256],
                                    stgw2[:, s4 * 256:(s4 + 1) * 256])
                                cast_l.append(inc(ins, "ve"))
                            sect("vector", f_c)

                        for j in range(32):
                            s4 = j % 4
                            if j < len(w2_dma):
                                pass
                            elif j < 4:
                                emit_w2_dc(j, e, og, w2_dma, w2_cast,
                                           state["prev_w2_cast"],
                                           state["prev_ffn_mm"] if og == 0 else None)
                            else:
                                emit_w2_dc(j, e, og, w2_dma, w2_cast,
                                           w2_cast[j - 4], w2_mm[j - 4])

                            def f_t(t, j=j, s4=s4, og=og, gp_par=gp_par):
                                t.wait_ge(sems["ve"], w2_cast[j])
                                if og == 0:
                                    t.wait_ge(sems["ve"], w1_hg[j])
                                if j == 0:
                                    if og >= 2:
                                        t.wait_ge(sems["ve"], ev_done[og - 2])
                                    else:
                                        t.wait_ge(sems["ac"], w1_silu[-1] if gp_par
                                                  else w1_silu[-2])
                                        t.wait_ge(sems["ve"], w1_hg[-1] if gp_par
                                                  else w1_hg[-2])
                                ins = None
                                for o2 in range(2):
                                    for tsl in range(2):
                                        ins = t.matmul(
                                            p_fo[gp_par * 4 + o2 * 2 + tsl][:, :],
                                            w2bf[:, s4 * 256 + o2 * 128:
                                                 s4 * 256 + (o2 + 1) * 128],
                                            A1[:, j * HT + tsl * 512:
                                               j * HT + (tsl + 1) * 512],
                                            start=(j == 0), stop=(j == 31))
                                w2_mm.append(inc(ins, "pe"))
                            sect("tensor", f_t)
                        prev_cast = w2_cast
                        nxt = None
                        if og < 3:
                            nxt = (e, og + 1)
                        elif e < NE - 1:
                            nxt = (e + 1, 0)
                        if nxt is not None:
                            n_dma, n_cast = [], []
                            for jn in range(4):
                                emit_w2_dc(jn, nxt[0], nxt[1], n_dma, n_cast,
                                           w2_cast[28 + jn], w2_mm[28 + jn])
                            if og < 3:
                                state["w2_pend_next"] = (n_dma, n_cast)
                            else:
                                state["w2_pend"] = (n_dma, n_cast)
                        if og == 2 and e < NE - 1:
                            pf_dma, pf_cast = [], []
                            for jp in range(2):
                                def f_qp(g, jp=jp, e=e):
                                    g.wait_ge(sems["ve"], w1_cast[30 + jp])
                                    ins = g.dma_start(
                                        out=stg[:, jp * 2048:(jp + 1) * 2048],
                                        in_=w1t[e + 1, :, jp * 2048:(jp + 1) * 2048])
                                    pf_dma.append(inc(ins, f"dW{jp}", 16))
                                sect("sync", f_qp)

                                def f_cp(v, jp=jp):
                                    v.wait_ge(sems[f"dW{jp}"], pf_dma[jp])
                                    v.wait_ge(sems["ve"], w1_hg[-1])
                                    ins = v.tensor_copy(
                                        wbf[:, jp * 2048:(jp + 1) * 2048],
                                        stg[:, jp * 2048:(jp + 1) * 2048])
                                    pf_cast.append(inc(ins, "ve"))
                                sect("vector", f_cp)
                            state["ffn_pref"] = (pf_dma, pf_cast)

                        def f_v2(v, e=e, og=og, gp_par=gp_par, wlast=w2_mm[-1],
                                 sb2_rdy=sb2_rdy):
                            v.wait_ge(sems["pe"], wlast)
                            v.wait_ge(sems["ve"], sb2_rdy)
                            ins = None
                            for o2 in range(2):
                                o = og * 2 + o2
                                for tsl in range(2):
                                    asl = acc[:, o * HT + tsl * 512:
                                              o * HT + (tsl + 1) * 512]
                                    ssl = s_b2[:, (e % 2) * HT + tsl * 512:
                                               (e % 2) * HT + (tsl + 1) * 512]
                                    v.tensor_tensor(tmp1024[:, 0:512],
                                                    p_fo[gp_par * 4 + o2 * 2 + tsl][:, :],
                                                    ssl, op=ALU.mult)
                                    v.drain()
                                    ins = v.tensor_tensor(asl, tmp1024[:, 0:512],
                                                          asl, op=ALU.add)
                                    v.drain()
                            ev_done.append(inc(ins, "ve"))
                            note("vector", ins)
                        sect("vector", f_v2)
                        state["prev_ffn_mm"] = w2_mm[-1]
                    state["prev_ffn_ev"] = ev_done[-1]
                    state["prev_ffn_ev_pair"] = [ev_done[-2], ev_done[-1]]
                    state["prev_w2_cast"] = prev_cast[-1]
                    state["sb2_guard"][e % 2] = ev_done[-1]
                    state["stg_cast_last"] = ("ve", w1_cast[-1])
                    state["prev_ffn_hg"] = w1_hg[-1]
            barrier()

        def store_phase(half):
            tgt = {s: C[s] for s in ("pe", "ve", "ac", "gp")}

            def f_g(g):
                for s in ("pe", "ve", "ac", "gp"):
                    if tgt[s] > 0:
                        g.wait_ge(sems[s], tgt[s])
                for cch in range(NKC):
                    inc(g.dma_start(
                        out=out_t[cch * 128:(cch + 1) * 128,
                                  half * HT:(half + 1) * HT],
                        in_=acc[:, cch * HT:(cch + 1) * HT]), "dS", 16)
            sect("gpsimd", f_g)

        init_phase()
        for half in range(n_half):
            for q in range(2):
                ln_phase(half, q)
                z_phase(half, q)
                gate_phase(half, q)
                ssd_phase(half, q)
                rms_wout_phase(half, q)
            router_phase(half)
            ffn_phase(half)
            store_phase(half)
        ds_t = C["dS"]
        sect("gpsimd", lambda g: g.wait_ge(sems["dS"], ds_t))

    return nc


def host_prep(inputs, n_half=2):
    LT = n_half * HT
    W_in = np.asarray(inputs["L0_attn_W_in"], dtype=np.float32)
    conv_w = np.asarray(inputs["L0_attn_conv_w"], dtype=np.float32)
    conv_b = np.asarray(inputs["L0_attn_conv_b"], dtype=np.float32)
    dp = np.asarray(inputs["L0_attn_delta_param"], dtype=np.float32)
    W_gate = np.asarray(inputs["L0_attn_W_gate"], dtype=np.float32)
    norm_w = np.asarray(inputs["L0_attn_norm_w"], dtype=np.float32)
    W_out = np.asarray(inputs["L0_attn_W_out"], dtype=np.float32)
    r_w = np.asarray(inputs["L0_ffn_router_w"], dtype=np.float32)
    r_b = np.asarray(inputs["L0_ffn_router_b"], dtype=np.float32)
    w1 = np.asarray(inputs["L0_ffn_w1"], dtype=np.float32)
    w2 = np.asarray(inputs["L0_ffn_w2"], dtype=np.float32)

    ctg = np.ascontiguousarray
    shared = {}
    wx = W_in[:, :2048].reshape(8, 128, 16, 128).transpose(2, 1, 0, 3) \
        .reshape(16, 128, 1024)
    dg = np.zeros((16, 128, 4, 128), dtype=np.float32)
    ii = np.arange(128)
    dg[:, ii, :, ii] = conv_w.reshape(16, 128, 4).transpose(1, 0, 2)
    shared["win_x"] = ctg(np.concatenate([wx, dg.reshape(16, 128, 512)], axis=2))
    shared["win_b"] = ctg(W_in[:, 2048:3072].reshape(8, 128, 8, 128)
                          .transpose(2, 1, 0, 3).reshape(8, 128, 1024))
    shared["win_c"] = ctg(W_in[:, 3072:4096].reshape(8, 128, 8, 128)
                          .transpose(2, 1, 0, 3).reshape(8, 128, 1024))
    shared["win_dt"] = ctg(W_in[:, 4096:4112].reshape(8, 128, 16)
                           .transpose(1, 0, 2).reshape(128, 128))
    shared["wgate"] = ctg(W_gate.reshape(8, 128, 16, 128)
                          .transpose(2, 1, 0, 3).reshape(16, 128, 1024))
    shared["wout"] = ctg(W_out.reshape(16, 128, 8, 128)
                         .transpose(2, 1, 0, 3).reshape(8, 128, 2048))
    shared["wrout"] = ctg(r_w.reshape(8, 128, 8).transpose(1, 0, 2).reshape(128, 64))
    order = np.empty(64, dtype=np.int64)
    order[0::2] = np.arange(32)
    order[1::2] = np.arange(32, 64)
    w1r = w1.reshape(8, 8, 128, 64, 128).transpose(0, 2, 3, 1, 4)
    shared["w1t"] = ctg(w1r[:, :, order].reshape(8, 128, 65536))
    w2r = w2.reshape(8, 32, 128, 4, 2, 128).transpose(0, 3, 2, 1, 4, 5)
    shared["w2t"] = ctg(w2r.reshape(8, 4, 128, 8192))
    ident = np.eye(128, dtype=np.float32)
    shared["c_idbf"] = ident.astype(BF)
    shared["c_idf"] = ident
    sidx = np.arange(128)
    shared["c_tri01"] = (sidx[:, None] <= sidx[None, :]).astype(np.float32)
    shared["c_trimask"] = (NEG_BIG * (sidx[:, None] > sidx[None, :])).astype(np.float32)
    shared["c_onesrow"] = np.ones((1, 128), dtype=np.float32)
    shared["c_onescol"] = np.ones((128, 1), dtype=np.float32).astype(BF)
    cc = np.zeros((128, 4), dtype=np.float32)
    cc[:, 0] = 1.0
    cc[:, 1] = LN_EPS
    cc[:, 2] = RMS_EPS
    shared["c_cc"] = cc
    shared["c_normw"] = ctg(norm_w.reshape(16, 128).T)
    shared["c_convb"] = ctg(conv_b.reshape(16, 128).T)
    shared["c_convw"] = ctg(conv_w.reshape(16, 128, 4).transpose(1, 0, 2)
                            .reshape(128, 64))
    shared["c_dptile"] = np.tile(dp, (128, 4)).astype(np.float32)
    shared["c_routb"] = r_b.reshape(8, 1)
    return shared


def kernel(**inputs):
    n_half = 2
    u = np.asarray(inputs["L0_attn_u"], dtype=np.float32)
    ncores = u.shape[0]
    shared = host_prep(inputs, n_half)
    nc = build_program(n_half)
    in_maps = []
    for c in range(ncores):
        m = dict(shared)
        m["u_t"] = np.ascontiguousarray(u[c].T)
        in_maps.append(m)
    res = run_bass_kernel_spmd(nc, in_maps, core_ids=list(range(ncores)))
    out = np.stack([np.ascontiguousarray(r["out_t"].T) for r in res.results], axis=0)
    return out
